# version 17
# baseline (speedup 1.0000x reference)
import numpy as np
from contextlib import ExitStack

import concourse.bass as bass
import concourse.mybir as mybir
from concourse.bass_utils import run_bass_kernel_spmd

F32 = mybir.dt.float32
BF16 = mybir.dt.bfloat16
F32R = mybir.dt.float32r
AF = mybir.ActivationFunctionType
ALU = mybir.AluOpType

T = 8192
Q = 512
NCH = T // Q
D = 1024
DI = 2048
NS = 16
MB = D // 128
DB = DI // 128
NG = 4
TSUB = 128
NTS = Q // TSUB

DEBUG_SCRATCH = False


class Ctr:

    def __init__(self, stack, nc, name):
        cm = nc.semaphore(name)
        self.h = cm.__enter__()
        stack.callback(cm.__exit__, None, None, None)
        self.v = 0

    def tick(self, n=1):
        self.v += n
        return self.v


_COUNTING = [True]


def inc(ctr, instr, n=1):
    instr.then_inc(ctr.h, n)
    if _COUNTING[0]:
        ctr.v += n
    return ctr.v


class _MockInstr:
    def then_inc(self, *a, **k):
        return self


class _MockEng:

    def __getattr__(self, name):
        def m(*a, **k):
            return _MockInstr()
        return m


_MOCK = _MockEng()

_ENGS = ("sync", "gpsimd", "vector", "scalar", "tensor")


class _Recorder:

    def __init__(self, real_block):
        self._rb = real_block
        self.q = {e: [] for e in _ENGS}

    def _mk(self, eng):
        def dec(fn):
            self.q[eng].append(fn)
            fn(_MOCK)
        return dec

    @property
    def sync(self):
        return self._mk("sync")

    @property
    def gpsimd(self):
        return self._mk("gpsimd")

    @property
    def vector(self):
        return self._mk("vector")

    @property
    def scalar(self):
        return self._mk("scalar")

    @property
    def tensor(self):
        return self._mk("tensor")

    def flush(self):
        _COUNTING[0] = False
        try:
            for name in _ENGS:
                fns = self.q[name]
                if not fns:
                    continue
                dec = getattr(self._rb, name)

                def _section(eng, fns=tuple(fns)):
                    for f in fns:
                        f(eng)
                dec(_section)
                self.q[name] = []
        finally:
            _COUNTING[0] = True


def build_core_program():
    nc = bass.Bass("TRN2", target_bir_lowering=False,
                   detect_race_conditions=False)

    xT = nc.dram_tensor("xT", [D, T], F32, kind="ExternalInput")
    w_in = nc.dram_tensor("w_in", [D, 2 * DI], F32, kind="ExternalInput")
    w_rT = nc.dram_tensor("w_rT", [D, D], F32, kind="ExternalInput")
    w_kT = nc.dram_tensor("w_kT", [D, D], F32, kind="ExternalInput")
    w_vT = nc.dram_tensor("w_vT", [D, D], F32, kind="ExternalInput")
    w_out = nc.dram_tensor("w_out", [DI, D], F32, kind="ExternalInput")
    w_x = nc.dram_tensor("w_x", [DI, 33], F32, kind="ExternalInput")
    wdt_row = nc.dram_tensor("wdt_row", [1, DI], F32, kind="ExternalInput")
    bdt_cols = nc.dram_tensor("bdt_cols", [128, DB], F32, kind="ExternalInput")
    convb_cols = nc.dram_tensor("convb_cols", [128, DB], F32, kind="ExternalInput")
    d_cols = nc.dram_tensor("d_cols", [128, DB], F32, kind="ExternalInput")
    convw_cols = nc.dram_tensor("convw_cols", [128, 4 * DB], F32, kind="ExternalInput")
    mur_cols = nc.dram_tensor("mur_cols", [128, MB], F32, kind="ExternalInput")
    muk_cols = nc.dram_tensor("muk_cols", [128, MB], F32, kind="ExternalInput")
    alog_row = nc.dram_tensor("alog_row", [1, NS], F32, kind="ExternalInput")

    outT = nc.dram_tensor("outT", [D, T], F32, kind="ExternalOutput")

    sk = "ExternalOutput" if DEBUG_SCRATCH else "Internal"
    xn_s = nc.dram_tensor("xn_s", [D, T], BF16, kind=sk)
    outp_s = nc.dram_tensor("outp_s", [D, T], F32, kind=sk)
    xi_s = nc.dram_tensor("xi_s", [DI, T], BF16, kind=sk)
    sz_s = nc.dram_tensor("sz_s", [DI, T], BF16, kind=sk)
    delta_s = nc.dram_tensor("delta_s", [DI, T], BF16, kind=sk)
    w_s = nc.dram_tensor("w_s", [DI, T], BF16, kind=sk)
    xp_s = nc.dram_tensor("xp_s", [33, T], F32, kind=sk)
    m_row_s = nc.dram_tensor("m_row_s", [1, T], F32, kind=sk)
    r_row_s = nc.dram_tensor("r_row_s", [1, T], F32, kind=sk)
    NIMG = NCH * NTS * NG * 2
    bd_s = nc.dram_tensor("bd_s", [NIMG, 128 * 256], BF16, kind=sk)
    id_s = nc.dram_tensor("id_s", [128, 128], F32, kind=sk)
    dd_s = nc.dram_tensor("dd_s", [DB, 128 * 128], BF16, kind=sk)
    convd_s = nc.dram_tensor("convd_s", [4 * DB, 128 * 128], BF16, kind=sk)

    with ExitStack() as S:
        E = S.enter_context
        block = E(nc.Block())

        sW = Ctr(S, nc, "sW")
        sG = Ctr(S, nc, "sG")
        sV = Ctr(S, nc, "sV")
        sS = Ctr(S, nc, "sS")
        sT = Ctr(S, nc, "sT")
        sP = Ctr(S, nc, "sP")
        sY = Ctr(S, nc, "sY")
        sR = Ctr(S, nc, "sR")
        sX = Ctr(S, nc, "sX")
        bar = Ctr(S, nc, "bar")

        onesinv_col = E(nc.sbuf_tensor("onesinv_col", [128, 1], BF16))
        a_rep = E(nc.sbuf_tensor("a_rep", [128, NS], F32))
        idb = E(nc.sbuf_tensor("idb", [128, 128], BF16))
        colbuf = E(nc.sbuf_tensor("colbuf", [128, 6 * DB], F32))
        OF_BDT, OF_CVB, OF_D = 0, DB, 2 * DB
        OF_MUR, OF_MUK = 3 * DB, 3 * DB + MB
        OF_OMR, OF_OMK = 3 * DB + 2 * MB, 3 * DB + 3 * MB
        wdtr_sb = E(nc.sbuf_tensor("wdtr_sb", [1, DI], F32R))
        nopt = E(nc.sbuf_tensor("nopt", [1, 2], F32))

        with ExitStack() as SP:
            EP = SP.enter_context
            zero_big = EP(nc.sbuf_tensor("zero_big", [128, 2048], F32))
            colbf = EP(nc.sbuf_tensor("colbf", [128, 5 * DB], BF16))
            idcol = EP(nc.sbuf_tensor("idcol", [128, 1], F32))
            idn_f = EP(nc.sbuf_tensor("idn_f", [128, 128], F32))
            wdt_f = EP(nc.sbuf_tensor("wdt_f", [1, DI], F32))

            @block.vector
            def _(v):
                v.memset(onesinv_col[:], 1.0 / 1024.0)
                v.memset(idcol[:], 1.0)
                v.memset(nopt[:], 0.0)
                inc(sV, v.memset(zero_big[:], 0.0))
            V_ZERO = sV.v

            @block.sync
            def _(s):
                inc(sW, s.dma_start(colbuf[:, OF_BDT:OF_BDT + DB], bdt_cols.ap()), 16)
                inc(sW, s.dma_start(colbuf[:, OF_CVB:OF_CVB + DB], convb_cols.ap()), 16)
                inc(sW, s.dma_start(colbuf[:, OF_D:OF_D + DB], d_cols.ap()), 16)
                inc(sW, s.dma_start(colbuf[:, OF_MUR:OF_MUR + MB], mur_cols.ap()), 16)
                inc(sW, s.dma_start(colbuf[:, OF_MUK:OF_MUK + MB], muk_cols.ap()), 16)
                inc(sW, s.dma_start(wdt_f[:], wdt_row.ap()), 16)
                s.wait_ge(sV.h, V_ZERO)
                inc(sW, s.dma_start(id_s.ap(), zero_big[:, 0:128]), 16)
                zb = zero_big[:, :].bitcast(BF16)
                for img in range(0, NIMG, 16):
                    inc(sW, s.dma_start(
                        bass.AP(bd_s, img * 32768, [[4096, 128], [1, 4096]]), zb), 16)
            W_PRE = sW.v

            @block.gpsimd
            def _(g):
                inc(sG, g.dma_start(
                    a_rep[:], bass.AP(alog_row, 0, [[0, 128], [1, NS]])), 16)
                g.wait_ge(sW.h, W_PRE)
                inc(sG, g.dma_start(colbf[:, 0:DB], d_cols.ap()), 16)
                inc(sG, g.dma_start(colbf[:, DB:5 * DB], convw_cols.ap()), 16)
                inc(sG, g.dma_start(bass.AP(id_s, 0, [[129, 128], [1, 1]]),
                                    idcol[:, :], allow_slow_non_contiguous=True), 16)
                for db in range(DB):
                    inc(sG, g.dma_start(
                        bass.AP(dd_s, db * 16384, [[129, 128], [1, 1]]),
                        colbf[:, db:db + 1], allow_slow_non_contiguous=True), 16)
                for j in range(4 * DB):
                    inc(sG, g.dma_start(
                        bass.AP(convd_s, j * 16384, [[129, 128], [1, 1]]),
                        colbf[:, DB + j:DB + j + 1],
                        allow_slow_non_contiguous=True), 16)
            G_PRE = sG.v

            @block.sync
            def _(s):
                s.wait_ge(sG.h, G_PRE)
                inc(sW, s.dma_start(idn_f[:], id_s.ap()), 16)
            W_IDN = sW.v

            @block.scalar
            def _(sc):
                sc.wait_ge(sG.h, G_PRE)
                inc(sS, sc.activation(a_rep[:], a_rep[:], AF.Exp))
            S_AREP = sS.v

            @block.vector
            def _(v):
                v.wait_ge(sS.h, S_AREP)
                v.tensor_scalar_mul(a_rep[:], a_rep[:], -1.0)
                v.wait_ge(sW.h, W_IDN)
                v.tensor_copy(idb[:], idn_f[:])
                v.tensor_copy(wdtr_sb[:], wdt_f[:])
                v.tensor_scalar(colbuf[:, OF_OMR:OF_OMR + MB],
                                colbuf[:, OF_MUR:OF_MUR + MB],
                                -1.0, 1.0, ALU.mult, ALU.add)
                inc(sV, v.tensor_scalar(colbuf[:, OF_OMK:OF_OMK + MB],
                                        colbuf[:, OF_MUK:OF_MUK + MB],
                                        -1.0, 1.0, ALU.mult, ALU.add))

        def emit_barrier():
            tgt = bar.v + 5

            @block.sync
            def _(s, wv=sW.v, tgt=tgt):
                s.wait_ge(sW.h, wv)
                s.sem_inc(bar.h, 1)
                s.wait_ge(bar.h, tgt)

            @block.gpsimd
            def _(g, gv=sG.v, tgt=tgt):
                g.wait_ge(sG.h, gv)
                g.sem_inc(bar.h, 1)
                g.wait_ge(bar.h, tgt)

            @block.vector
            def _(v):
                v.sem_inc(bar.h, 1)
                v.wait_ge(bar.h, tgt)

            @block.scalar
            def _(sc):
                sc.sem_inc(bar.h, 1)
                sc.wait_ge(bar.h, tgt)

            @block.tensor
            def _(t):
                t.sem_inc(bar.h, 1)
                t.wait_ge(bar.h, tgt)
            bar.v = tgt

        emit_barrier()

        real_block = block
        block = _Recorder(real_block)

        with ExitStack() as SA:
            EA = SA.enter_context
            wr_sb = EA(nc.sbuf_tensor("wr_sb", [128, MB * D], BF16))
            wk_sb = EA(nc.sbuf_tensor("wk_sb", [128, MB * D], BF16))
            wv_sb = EA(nc.sbuf_tensor("wv_sb", [128, MB * D], BF16))
            x_sb = EA(nc.sbuf_tensor("x_sb", [128, MB * Q], F32))
            xbf = EA(nc.sbuf_tensor("xbf", [128, MB * Q], BF16))
            x2bf = EA(nc.sbuf_tensor("x2bf", [128, MB * Q], BF16))
            xn_sb = EA(nc.sbuf_tensor("xn_sb", [128, MB * (Q + 1)], BF16))
            xr_sb = EA(nc.sbuf_tensor("xr_sb", [128, MB * Q], BF16))
            xk_sb = EA(nc.sbuf_tensor("xk_sb", [128, MB * Q], BF16))
            r_sb = EA(nc.sbuf_tensor("r_sb", [128, MB * Q], F32))
            kact_sb = EA(nc.sbuf_tensor("kact_sb", [128, MB * Q], BF16))
            krelu = EA(nc.sbuf_tensor("krelu", [128, Q], F32))
            mb_b = EA(nc.sbuf_tensor("mb_b", [128, Q], F32))
            rb_b = EA(nc.sbuf_tensor("rb_b", [128, Q], F32))
            strow = EA(nc.sbuf_tensor("strow", [1, 5 * Q], F32))
            out_sb = EA(nc.sbuf_tensor("out_sb", [128, MB * Q], F32))
            tmpA = EA(nc.sbuf_tensor("tmpA", [128, Q], F32))
            pA = EA(nc.psum_tensor("pA", [128, 2 * Q], F32))
            pM = EA(nc.psum_tensor("pM", [1, Q], F32))
            pM2 = EA(nc.psum_tensor("pM2", [1, Q], F32))

            @block.gpsimd
            def _(g):
                for kt in range(MB):
                    inc(sG, g.dma_start(wr_sb[:, kt * D:(kt + 1) * D],
                                        w_rT.ap()[kt * 128:(kt + 1) * 128, :]), 16)
                    inc(sG, g.dma_start(wk_sb[:, kt * D:(kt + 1) * D],
                                        w_kT.ap()[kt * 128:(kt + 1) * 128, :]), 16)
                    inc(sG, g.dma_start(wv_sb[:, kt * D:(kt + 1) * D],
                                        w_vT.ap()[kt * 128:(kt + 1) * 128, :]), 16)
            G_AW = sG.v

            @block.vector
            def _(v):
                for mb in range(MB):
                    v.memset(xn_sb[:, mb * (Q + 1):mb * (Q + 1) + 1], 0.0)
                inc(sV, v.memset(tmpA[:], 0.0))
            V_AZ = sV.v

            ffn_bank_free = [0, 0]
            xn_store_tgt = [0]
            out_store_tgt = [0]
            x_consumed = [0]

            for k in range(NCH):
                t0 = k * Q

                @block.sync
                def _(s, k=k, t0=t0, xc=x_consumed[0]):
                    if k == 0:
                        s.wait_ge(sG.h, G_AW)
                        s.wait_ge(sV.h, V_AZ)
                    else:
                        s.wait_ge(sV.h, xc)
                    for mb in range(MB):
                        inc(sW, s.dma_start(x_sb[:, mb * Q:(mb + 1) * Q],
                                            xT.ap()[mb * 128:(mb + 1) * 128, t0:t0 + Q]), 16)
                W_X = sW.v

                @block.vector
                def _(v, W_X=W_X):
                    v.wait_ge(sW.h, W_X)
                    for mb in range(MB):
                        v.tensor_copy(xbf[:, mb * Q:(mb + 1) * Q],
                                      x_sb[:, mb * Q:(mb + 1) * Q])
                    inc(sV, v.engine_nop())
                V_XBF = sV.v

                @block.scalar
                def _(sc, V_XBF=V_XBF):
                    sc.wait_ge(sV.h, V_XBF)
                    for mb in range(MB - 1):
                        sc.activation(x2bf[:, mb * Q:(mb + 1) * Q],
                                      xbf[:, mb * Q:(mb + 1) * Q], AF.Square)
                    inc(sS, sc.activation(x2bf[:, (MB - 1) * Q:MB * Q],
                                          xbf[:, (MB - 1) * Q:MB * Q], AF.Square))
                S_X2 = sS.v

                @block.tensor
                def _(t, S_X2=S_X2):
                    t.wait_ge(sS.h, S_X2)
                    for mb in range(MB):
                        t.matmul(pM[:, :], onesinv_col[:], xbf[:, mb * Q:(mb + 1) * Q],
                                 start=(mb == 0), stop=(mb == MB - 1))
                    for mb in range(MB):
                        mm = t.matmul(pM2[:, :], onesinv_col[:],
                                      x2bf[:, mb * Q:(mb + 1) * Q],
                                      start=(mb == 0), stop=(mb == MB - 1))
                    inc(sT, mm)
                T_STAT = sT.v

                @block.vector
                def _(v, T_STAT=T_STAT):
                    v.wait_ge(sT.h, T_STAT)
                    m = strow[0:1, 0:Q]
                    var = strow[0:1, 2 * Q:3 * Q]
                    t1 = strow[0:1, 3 * Q:4 * Q]
                    v.tensor_copy(m, pM[:, :])
                    v.tensor_tensor(t1, m, m, ALU.mult)
                    v.tensor_tensor(var, pM2[:, :], t1, ALU.subtract)
                    inc(sV, v.tensor_scalar_add(var, var, 1e-5))
                V_VAR = sV.v

                @block.scalar
                def _(sc, V_VAR=V_VAR):
                    sc.wait_ge(sV.h, V_VAR)
                    var = strow[0:1, 2 * Q:3 * Q]
                    t1 = strow[0:1, 3 * Q:4 * Q]
                    sc.activation(t1, var, AF.Ln)
                    inc(sS, sc.activation(t1, t1, AF.Exp, scale=-0.5))
                S_RS = sS.v

                @block.vector
                def _(v, S_RS=S_RS):
                    v.wait_ge(sS.h, S_RS)
                    rr = strow[0:1, Q:2 * Q]
                    var = strow[0:1, 2 * Q:3 * Q]
                    r0 = strow[0:1, 3 * Q:4 * Q]
                    t2 = strow[0:1, 4 * Q:5 * Q]
                    v.tensor_tensor(t2, r0, r0, ALU.mult)
                    v.tensor_tensor(t2, t2, var, ALU.mult)
                    v.tensor_scalar(t2, t2, -0.5, 1.5, ALU.mult, ALU.add)
                    inc(sV, v.tensor_tensor(rr, r0, t2, ALU.mult))
                V_R = sV.v

                @block.sync
                def _(s, t0=t0, V_R=V_R):
                    s.wait_ge(sV.h, V_R)
                    inc(sW, s.dma_start(m_row_s.ap()[0:1, t0:t0 + Q], strow[0:1, 0:Q]), 16)
                    inc(sW, s.dma_start(r_row_s.ap()[0:1, t0:t0 + Q], strow[0:1, Q:2 * Q]), 16)
                W_MR = sW.v

                @block.gpsimd
                def _(g, t0=t0, W_MR=W_MR):
                    g.wait_ge(sW.h, W_MR)
                    inc(sG, g.dma_start(mb_b[:], bass.AP(m_row_s, t0, [[0, 128], [1, Q]])), 16)
                    inc(sG, g.dma_start(rb_b[:], bass.AP(r_row_s, t0, [[0, 128], [1, Q]])), 16)
                G_MR = sG.v

                @block.vector
                def _(v, G_MR=G_MR, k=k, xst=xn_store_tgt[0]):
                    v.wait_ge(sG.h, G_MR)
                    if k > 0:
                        v.wait_ge(sW.h, xst)
                    for mb in range(MB):
                        sl = xn_sb[:, mb * (Q + 1):(mb + 1) * (Q + 1)]
                        if k > 0:
                            v.tensor_copy(sl[:, 0:1], sl[:, Q:Q + 1])
                        v.tensor_tensor(tmpA[:], x_sb[:, mb * Q:(mb + 1) * Q], mb_b[:],
                                        ALU.subtract)
                        v.tensor_tensor(sl[:, 1:Q + 1], tmpA[:], rb_b[:], ALU.mult)
                    for mb in range(MB):
                        sl = xn_sb[:, mb * (Q + 1):(mb + 1) * (Q + 1)]
                        v.tensor_scalar_mul(xr_sb[:, mb * Q:(mb + 1) * Q], sl[:, 1:Q + 1],
                                            colbuf[:, OF_MUR + mb:OF_MUR + mb + 1])
                        v.scalar_tensor_tensor(xr_sb[:, mb * Q:(mb + 1) * Q], sl[:, 0:Q],
                                               colbuf[:, OF_OMR + mb:OF_OMR + mb + 1],
                                               xr_sb[:, mb * Q:(mb + 1) * Q],
                                               ALU.mult, ALU.add)
                        v.tensor_scalar_mul(xk_sb[:, mb * Q:(mb + 1) * Q], sl[:, 1:Q + 1],
                                            colbuf[:, OF_MUK + mb:OF_MUK + mb + 1])
                        v.scalar_tensor_tensor(xk_sb[:, mb * Q:(mb + 1) * Q], sl[:, 0:Q],
                                               colbuf[:, OF_OMK + mb:OF_OMK + mb + 1],
                                               xk_sb[:, mb * Q:(mb + 1) * Q],
                                               ALU.mult, ALU.add)
                    inc(sV, v.engine_nop())
                V_XN = sV.v

                @block.sync
                def _(s, t0=t0, V_XN=V_XN):
                    s.wait_ge(sV.h, V_XN)
                    for mb in range(MB):
                        inc(sW, s.dma_start(xn_s.ap()[mb * 128:(mb + 1) * 128, t0:t0 + Q],
                                            xn_sb[:, mb * (Q + 1) + 1:(mb + 1) * (Q + 1)]), 16)
                xn_store_tgt[0] = sW.v

                sig_done = {}
                for mo in range(MB):
                    bank = mo % 2

                    @block.tensor
                    def _(t, mo=mo, bank=bank, V_XN=V_XN,
                          free=ffn_bank_free[bank]):
                        t.wait_ge(sV.h, V_XN)
                        if free:
                            t.wait_ge(sS.h, free)
                        for kt in range(MB):
                            mm = t.matmul(pA[:, bank * Q:(bank + 1) * Q],
                                          wr_sb[:, kt * D + mo * 128:kt * D + (mo + 1) * 128],
                                          xr_sb[:, kt * Q:(kt + 1) * Q],
                                          start=(kt == 0), stop=(kt == MB - 1))
                        inc(sT, mm)
                    T_R = sT.v

                    @block.scalar
                    def _(sc, mo=mo, bank=bank, T_R=T_R):
                        sc.wait_ge(sT.h, T_R)
                        inc(sS, sc.activation(r_sb[:, mo * Q:(mo + 1) * Q],
                                              pA[:, bank * Q:(bank + 1) * Q], AF.Sigmoid))
                    ffn_bank_free[bank] = sS.v

                for mo in range(MB):
                    bank = mo % 2

                    @block.tensor
                    def _(t, mo=mo, bank=bank, free=ffn_bank_free[bank]):
                        t.wait_ge(sS.h, free)
                        for kt in range(MB):
                            mm = t.matmul(pA[:, bank * Q:(bank + 1) * Q],
                                          wk_sb[:, kt * D + mo * 128:kt * D + (mo + 1) * 128],
                                          xk_sb[:, kt * Q:(kt + 1) * Q],
                                          start=(kt == 0), stop=(kt == MB - 1))
                        inc(sT, mm)
                    T_K = sT.v

                    @block.scalar
                    def _(sc, mo=mo, bank=bank, T_K=T_K):
                        sc.wait_ge(sT.h, T_K)
                        sc.activation(krelu[:], pA[:, bank * Q:(bank + 1) * Q], AF.Relu)
                        inc(sS, sc.activation(kact_sb[:, mo * Q:(mo + 1) * Q], krelu[:],
                                              AF.Square))
                    ffn_bank_free[bank] = sS.v

                kact_all = max(ffn_bank_free)
                v_bank_free = [kact_all, kact_all]
                v_bank_is_vec = [False, False]
                for mo in range(MB):
                    bank = mo % 2

                    @block.tensor
                    def _(t, mo=mo, bank=bank, free=v_bank_free[bank],
                          isv=v_bank_is_vec[bank], ka=kact_all):
                        t.wait_ge(sS.h, ka)
                        t.wait_ge(sV.h if isv else sS.h, free)
                        for kt in range(MB):
                            mm = t.matmul(pA[:, bank * Q:(bank + 1) * Q],
                                          wv_sb[:, kt * D + mo * 128:kt * D + (mo + 1) * 128],
                                          kact_sb[:, kt * Q:(kt + 1) * Q],
                                          start=(kt == 0), stop=(kt == MB - 1))
                        inc(sT, mm)
                    T_V = sT.v

                    @block.vector
                    def _(v, mo=mo, bank=bank, T_V=T_V, ost=out_store_tgt[0]):
                        v.wait_ge(sT.h, T_V)
                        if ost:
                            v.wait_ge(sW.h, ost)
                        v.tensor_tensor(tmpA[:], r_sb[:, mo * Q:(mo + 1) * Q],
                                        pA[:, bank * Q:(bank + 1) * Q], ALU.mult)
                        inc(sV, v.tensor_tensor(out_sb[:, mo * Q:(mo + 1) * Q], tmpA[:],
                                                x_sb[:, mo * Q:(mo + 1) * Q], ALU.add))
                    v_bank_free[bank] = sV.v
                    v_bank_is_vec[bank] = True
                V_OUT = sV.v
                x_consumed[0] = V_OUT

                @block.sync
                def _(s, t0=t0, V_OUT=V_OUT):
                    s.wait_ge(sV.h, V_OUT)
                    for mb in range(MB):
                        inc(sW, s.dma_start(outp_s.ap()[mb * 128:(mb + 1) * 128, t0:t0 + Q],
                                            out_sb[:, mb * Q:(mb + 1) * Q]), 16)
                out_store_tgt[0] = sW.v

            emit_barrier()
            block.flush()

        with ExitStack() as SB:
            EB = SB.enter_context
            win_sb = EB(nc.sbuf_tensor("win_sb", [128, MB * 2 * DI], BF16))
            wx_sb = EB(nc.sbuf_tensor("wx_sb", [128, DB * 33], BF16))
            convd_sb = EB(nc.sbuf_tensor("convd_sb", [128, 4 * DB * 128], BF16))
            xnb_sb = EB(nc.sbuf_tensor("xnb_sb", [128, MB * Q], BF16))
            xipre = EB(nc.sbuf_tensor("xipre", [128, DB * (Q + 3)], BF16))
            xib_sb = EB(nc.sbuf_tensor("xib_sb", [128, DB * Q], BF16))
            szb = EB(nc.sbuf_tensor("szb", [128, 2 * Q], BF16))
            xp_sb = EB(nc.sbuf_tensor("xp_sb", [33, Q], F32))
            xp0r = EB(nc.sbuf_tensor("xp0r", [1, Q], F32R))
            xpt_sb = EB(nc.sbuf_tensor("xpt_sb", [128, NTS * 33], BF16))
            dtmp = EB(nc.sbuf_tensor("dtmp", [128, 2 * Q], F32))
            wtmp = EB(nc.sbuf_tensor("wtmp", [128, 2 * Q], BF16))
            pB = EB(nc.psum_tensor("pB", [128, 2 * Q], F32))
            pBc = EB(nc.psum_tensor("pBc", [128, 2 * Q], F32))
            pXp = EB(nc.psum_tensor("pXp", [33, Q], F32))
            pXpt = EB(nc.psum_tensor("pXpt", [128, NTS * 33], F32))

            @block.gpsimd
            def _(g):
                for kt in range(MB):
                    inc(sG, g.dma_start(win_sb[:, kt * 2 * DI:(kt + 1) * 2 * DI],
                                        w_in.ap()[kt * 128:(kt + 1) * 128, :]), 16)
                for kt in range(DB):
                    inc(sG, g.dma_start(wx_sb[:, kt * 33:(kt + 1) * 33],
                                        w_x.ap()[kt * 128:(kt + 1) * 128, :]), 16)
            G_BW = sG.v

            @block.sync
            def _(s):
                for j in range(4 * DB):
                    inc(sW, s.dma_start(
                        convd_sb[:, j * 128:(j + 1) * 128],
                        bass.AP(convd_s, j * 16384, [[128, 128], [1, 128]])), 16)
            W_BCD = sW.v

            @block.vector
            def _(v):
                for db in range(DB):
                    v.memset(xipre[:, db * (Q + 3):db * (Q + 3) + 3], 0.0)
                inc(sV, v.engine_nop())
            V_BZ = sV.v

            evac_tgt = [0, 0]
            cv_tgt = [0, 0]
            sz_store = [0, 0]
            d_store = [None, None]
            w_store = [0, 0]
            w_mult = [0, 0]
            xn_consumed = [0]
            bdw_tgt = [0]
            xi_store_tgt = [0]
            w_mult_last = [0]

            for k in range(NCH):
                t0 = k * Q

                @block.sync
                def _(s, k=k, t0=t0, xc=xn_consumed[0]):
                    if k == 0:
                        s.wait_ge(sG.h, G_BW)
                        s.wait_ge(sV.h, V_BZ)
                        s.wait_ge(sW.h, W_BCD)
                    else:
                        s.wait_ge(sT.h, xc)
                    for mb in range(MB):
                        inc(sW, s.dma_start(xnb_sb[:, mb * Q:(mb + 1) * Q],
                                            xn_s.ap()[mb * 128:(mb + 1) * 128, t0:t0 + Q]), 16)
                W_XN = sW.v

                @block.scalar
                def _(sc, k=k):
                    if k > 0:
                        for db in range(DB):
                            sl = xipre[:, db * (Q + 3):(db + 1) * (Q + 3)]
                            sc.copy(sl[:, 0:3], sl[:, Q:Q + 3])
                    inc(sS, sc.copy(nopt[0:1, 0:1], nopt[0:1, 1:2]))
                S_HIST = sS.v

                for o in range(2 * DB):
                    bank = o % 2

                    @block.tensor
                    def _(t, o=o, bank=bank, W_XN=W_XN, free=evac_tgt[bank]):
                        t.wait_ge(sW.h, W_XN)
                        if free:
                            t.wait_ge(sS.h, free)
                        for kt in range(MB):
                            mm = t.matmul(pB[:, bank * Q:(bank + 1) * Q],
                                          win_sb[:, kt * 2 * DI + o * 128:kt * 2 * DI + (o + 1) * 128],
                                          xnb_sb[:, kt * Q:(kt + 1) * Q],
                                          start=(kt == 0), stop=(kt == MB - 1))
                        inc(sT, mm)
                    T_O = sT.v

                    @block.scalar
                    def _(sc, o=o, bank=bank, T_O=T_O, S_HIST=S_HIST,
                          szst=sz_store[o % 2]):
                        sc.wait_ge(sT.h, T_O)
                        if o < DB:
                            sc.wait_ge(sS.h, S_HIST)
                            sl = xipre[:, o * (Q + 3):(o + 1) * (Q + 3)]
                            inc(sS, sc.copy(sl[:, 3:Q + 3], pB[:, bank * Q:(bank + 1) * Q]))
                        else:
                            if szst:
                                sc.wait_ge(sW.h, szst)
                            inc(sS, sc.activation(szb[:, (o % 2) * Q:(o % 2 + 1) * Q],
                                                  pB[:, bank * Q:(bank + 1) * Q], AF.Silu))
                    evac_tgt[bank] = sS.v

                    if o >= DB:
                        S_SZ = sS.v

                        @block.sync
                        def _(s, o=o, t0=t0, S_SZ=S_SZ):
                            s.wait_ge(sS.h, S_SZ)
                            db = o - DB
                            inc(sW, s.dma_start(
                                sz_s.ap()[db * 128:(db + 1) * 128, t0:t0 + Q],
                                szb[:, (o % 2) * Q:(o % 2 + 1) * Q]), 16)
                        sz_store[o % 2] = sW.v

                for db in range(DB):
                    bank = db % 2

                    @block.tensor
                    def _(t, db=db, bank=bank, free=cv_tgt[bank], ev=max(evac_tgt)):
                        t.wait_ge(sS.h, ev)
                        if free:
                            t.wait_ge(sS.h, free)
                        sl = xipre[:, db * (Q + 3):(db + 1) * (Q + 3)]
                        for j in range(4):
                            mm = t.matmul(pBc[:, bank * Q:(bank + 1) * Q],
                                          convd_sb[:, (j * DB + db) * 128:(j * DB + db + 1) * 128],
                                          sl[:, j:j + Q],
                                          start=(j == 0), stop=(j == 3))
                        inc(sT, mm)
                    T_CV = sT.v

                    @block.scalar
                    def _(sc, db=db, bank=bank, T_CV=T_CV,
                          xst=xi_store_tgt[0], wml=w_mult_last[0]):
                        sc.wait_ge(sT.h, T_CV)
                        if xst:
                            sc.wait_ge(sW.h, xst)
                        if wml:
                            sc.wait_ge(sV.h, wml)
                        inc(sS, sc.activation(xib_sb[:, db * Q:(db + 1) * Q],
                                              pBc[:, bank * Q:(bank + 1) * Q], AF.Silu,
                                              bias=colbuf[:, OF_CVB + db:OF_CVB + db + 1]))
                    cv_tgt[bank] = sS.v
                S_XI = sS.v

                @block.sync
                def _(s, t0=t0, S_XI=S_XI):
                    s.wait_ge(sS.h, S_XI)
                    for db in range(DB):
                        inc(sW, s.dma_start(xi_s.ap()[db * 128:(db + 1) * 128, t0:t0 + Q],
                                            xib_sb[:, db * Q:(db + 1) * Q]), 16)
                xi_store_tgt[0] = sW.v

                @block.tensor
                def _(t, S_XI=S_XI):
                    t.wait_ge(sS.h, S_XI)
                    for kt in range(DB):
                        mm = t.matmul(pXp[:, :], wx_sb[:, kt * 33:(kt + 1) * 33],
                                      xib_sb[:, kt * Q:(kt + 1) * Q],
                                      start=(kt == 0), stop=(kt == DB - 1))
                    inc(sT, mm)
                    for ts_ in range(NTS):
                        for kt in range(DB):
                            mm = t.matmul(pXpt[:, ts_ * 33:(ts_ + 1) * 33],
                                          xib_sb[:, kt * Q + ts_ * TSUB:kt * Q + (ts_ + 1) * TSUB],
                                          wx_sb[:, kt * 33:(kt + 1) * 33],
                                          start=(kt == 0), stop=(kt == DB - 1))
                    inc(sT, mm)
                T_XP = sT.v

                @block.scalar
                def _(sc, T_XP=T_XP, bdw=bdw_tgt[0]):
                    sc.wait_ge(sT.h, T_XP)
                    if bdw:
                        sc.wait_ge(sW.h, bdw)
                    sc.copy(xp_sb[:, :], pXp[:, :])
                    for ts_ in range(NTS):
                        inc(sS, sc.copy(xpt_sb[:, ts_ * 33:(ts_ + 1) * 33],
                                        pXpt[:, ts_ * 33:(ts_ + 1) * 33]))
                S_XP = sS.v

                @block.vector
                def _(v, S_XP=S_XP):
                    v.wait_ge(sS.h, S_XP)
                    inc(sV, v.tensor_copy(xp0r[:], xp_sb[0:1, :]))
                V_XP0 = sV.v

                @block.sync
                def _(s, t0=t0, S_XP=S_XP):
                    s.wait_ge(sS.h, S_XP)
                    inc(sW, s.dma_start(xp_s.ap()[:, t0:t0 + Q], xp_sb[:, :]), 16)

                @block.sync
                def _(s, k=k, S_XP=S_XP):
                    s.wait_ge(sS.h, S_XP)
                    for ts_ in range(NTS):
                        for gg in range(NG):
                            for hh in range(2):
                                img = ((k * NTS + ts_) * NG + gg) * 2 + hh
                                c0 = ts_ * 33 + 1 + 4 * gg + 2 * hh
                                inc(sW, s.dma_start(
                                    bass.AP(bd_s, img * 32768, [[258, 128], [1, 2]]),
                                    xpt_sb[:, c0:c0 + 2]), 16)
                bdw_tgt[0] = sW.v

                for db in range(DB):
                    slot = db % 2

                    @block.tensor
                    def _(t, db=db, slot=slot, V_XP0=V_XP0,
                          dst=d_store[slot], wm=w_mult[slot], cvt=max(cv_tgt)):
                        t.wait_ge(sV.h, V_XP0)
                        t.wait_ge(sS.h, cvt)
                        if wm:
                            t.wait_ge(sV.h, wm)
                        inc(sT, t.matmul(pBc[:, slot * Q:(slot + 1) * Q],
                                         wdtr_sb[0:1, db * 128:(db + 1) * 128],
                                         xp0r[:], start=True, stop=True))
                    T_DP = sT.v

                    @block.scalar
                    def _(sc, db=db, slot=slot, T_DP=T_DP, dst=d_store[slot]):
                        sc.wait_ge(sT.h, T_DP)
                        if dst:
                            sc.wait_ge(sW.h, dst[0])
                            sc.wait_ge(sG.h, dst[1])
                        sc.activation(dtmp[:, slot * Q:(slot + 1) * Q],
                                      pBc[:, slot * Q:(slot + 1) * Q], AF.Exp,
                                      bias=colbuf[:, OF_BDT + db:OF_BDT + db + 1])
                        inc(sS, sc.activation(dtmp[:, slot * Q:(slot + 1) * Q],
                                              dtmp[:, slot * Q:(slot + 1) * Q], AF.Ln,
                                              bias=1.0))
                    S_DL = sS.v

                    @block.vector
                    def _(v, db=db, slot=slot, S_DL=S_DL, wst=w_store[slot]):
                        v.wait_ge(sS.h, S_DL)
                        if wst:
                            v.wait_ge(sW.h, wst)
                        inc(sV, v.tensor_tensor(wtmp[:, slot * Q:(slot + 1) * Q],
                                                dtmp[:, slot * Q:(slot + 1) * Q],
                                                xib_sb[:, db * Q:(db + 1) * Q], ALU.mult))
                    w_mult[slot] = sV.v
                    V_W = sV.v

                    @block.gpsimd
                    def _(g, db=db, slot=slot, t0=t0, S_DL=S_DL):
                        g.wait_ge(sS.h, S_DL)
                        inc(sG, g.dma_start(delta_s.ap()[db * 128:(db + 1) * 128, t0:t0 + Q],
                                            dtmp[:, slot * Q:(slot + 1) * Q]), 16)
                    G_DST = sG.v

                    @block.sync
                    def _(s, db=db, slot=slot, t0=t0, V_W=V_W):
                        s.wait_ge(sV.h, V_W)
                        inc(sW, s.dma_start(w_s.ap()[db * 128:(db + 1) * 128, t0:t0 + Q],
                                            wtmp[:, slot * Q:(slot + 1) * Q]), 16)
                    d_store[slot] = (sW.v, G_DST)
                    w_store[slot] = sW.v

                xn_consumed[0] = sT.v
                w_mult_last[0] = sV.v

            emit_barrier()
            block.flush()

        with ExitStack() as SC:
            EC = SC.enter_context
            wout_sb = EC(nc.sbuf_tensor("wout_sb", [128, DB * D], BF16))
            dd_sb = EC(nc.sbuf_tensor("dd_sb", [128, DB * 128], BF16))
            delta_sb = EC(nc.sbuf_tensor("delta_sb", [128, DB * Q], BF16))
            xib2 = EC(nc.sbuf_tensor("xib2", [128, 2 * Q], BF16))
            szb2 = EC(nc.sbuf_tensor("szb2", [128, DB * Q], BF16))
            wt_sb = EC(nc.sbuf_tensor("wt_sb", [128, NTS * DI], BF16))
            bd_sb = EC(nc.sbuf_tensor("bd_sb", [128, NTS * NG * Q], BF16))
            cb_sb = EC(nc.sbuf_tensor("cb_sb", [128, NS * Q], BF16))
            a_sb = EC(nc.sbuf_tensor("a_sb", [128, 2 * NG * Q], F32))
            h_sb = EC(nc.sbuf_tensor("h_sb", [128, 2 * NG * Q], F32))
            p_sb = EC(nc.sbuf_tensor("p_sb", [128, 2 * NG * Q], BF16))
            ya_sb = EC(nc.sbuf_tensor("ya_sb", [128, 2 * DB * Q], BF16))
            outp_sb = EC(nc.sbuf_tensor("outp_sb", [128, 2 * Q], F32))
            outc_sb = EC(nc.sbuf_tensor("outc_sb", [128, 2 * Q], F32))
            state_sb = EC(nc.sbuf_tensor("state_sb", [128, DB * NS], F32))
            pU = EC(nc.psum_tensor("pU", [128, 3 * 1024], F32))
            pY = EC(nc.psum_tensor("pY", [128, Q], F32))
            pAt = EC(nc.psum_tensor("pAt", [128, Q], F32))

            @block.gpsimd
            def _(g):
                for kt in range(DB):
                    inc(sG, g.dma_start(wout_sb[:, kt * D:(kt + 1) * D],
                                        w_out.ap()[kt * 128:(kt + 1) * 128, :]), 16)
            G_CW = sG.v

            @block.sync
            def _(s):
                for db in range(DB):
                    inc(sW, s.dma_start(
                        dd_sb[:, db * 128:(db + 1) * 128],
                        bass.AP(dd_s, db * 16384, [[128, 128], [1, 128]])), 16)
            W_CD = sW.v

            @block.vector
            def _(v):
                inc(sV, v.memset(state_sb[:], 0.0))
            V_ST0 = sV.v

            out_store = [0, 0]
            h_free_par = [0, 0]
            yred_par = [0, 0]
            a_free_par = [0, 0]
            u_free_set = [0, 0, 0]
            tail_prev = None
            at_free = [0]
            op_free = [0, 0]
            pending_wout = [None]

            def emit_wout_stage(mo, t0k, V_YAk, YOFk):
                slot = mo % 2

                @block.sync
                def _(s, mo=mo, slot=slot, t0k=t0k, opf=op_free[slot]):
                    if opf:
                        s.wait_ge(sV.h, opf)
                    inc(sW, s.dma_start(outp_sb[:, slot * Q:(slot + 1) * Q],
                                        outp_s.ap()[mo * 128:(mo + 1) * 128, t0k:t0k + Q]), 16)
                W_OP = sW.v

                @block.tensor
                def _(t, mo=mo, V_YAk=V_YAk, YOFk=YOFk, atf=at_free[0]):
                    t.wait_ge(sY.h, V_YAk)
                    if atf:
                        t.wait_ge(sV.h, atf)
                    for kt in range(DB):
                        mm = t.matmul(pAt[:, :],
                                      wout_sb[:, kt * D + mo * 128:kt * D + (mo + 1) * 128],
                                      ya_sb[:, YOFk + kt * Q:YOFk + (kt + 1) * Q],
                                      start=(kt == 0), stop=(kt == DB - 1))
                    inc(sT, mm)
                T_AT = sT.v

                @block.vector
                def _(v, slot=slot, T_AT=T_AT, osf=out_store[slot], W_OP=W_OP):
                    v.wait_ge(sT.h, T_AT)
                    v.wait_ge(sW.h, W_OP)
                    if osf:
                        v.wait_ge(sW.h, osf)
                    inc(sV, v.tensor_tensor(outc_sb[:, slot * Q:(slot + 1) * Q],
                                            pAt[:, :],
                                            outp_sb[:, slot * Q:(slot + 1) * Q], ALU.add))
                V_OC = sV.v
                at_free[0] = V_OC
                op_free[slot] = V_OC

                @block.sync
                def _(s, mo=mo, slot=slot, t0k=t0k, V_OC=V_OC):
                    s.wait_ge(sV.h, V_OC)
                    inc(sW, s.dma_start(
                        outT.ap()[mo * 128:(mo + 1) * 128, t0k:t0k + Q],
                        outc_sb[:, slot * Q:(slot + 1) * Q]), 16)
                out_store[slot] = sW.v

            for k in range(NCH):
                t0 = k * Q

                @block.sync
                def _(s, k=k, t0=t0, tp=tail_prev):
                    if k == 0:
                        s.wait_ge(sG.h, G_CW)
                        s.wait_ge(sW.h, W_CD)
                        s.wait_ge(sV.h, V_ST0)
                    else:
                        s.wait_ge(sS.h, tp[0])
                        s.wait_ge(sT.h, tp[1])
                        s.wait_ge(sY.h, tp[2])
                    for ts_ in range(NTS):
                        inc(sX, s.dma_start_transpose(
                            wt_sb[:, ts_ * DI:(ts_ + 1) * DI],
                            w_s.ap()[:, t0 + ts_ * TSUB:t0 + (ts_ + 1) * TSUB]), 16)
                    for ts_ in range(NTS):
                        for gg in range(NG):
                            for hh in range(2):
                                img = ((k * NTS + ts_) * NG + gg) * 2 + hh
                                sl = ((ts_ * NG + gg) * 2 + hh) * 256
                                inc(sW, s.dma_start(
                                    bd_sb[:, sl:sl + 256],
                                    bass.AP(bd_s, img * 32768, [[256, 128], [1, 256]])), 16)
                    for db in range(4):
                        inc(sW, s.dma_start(delta_sb[:, db * Q:(db + 1) * Q],
                                            delta_s.ap()[db * 128:(db + 1) * 128, t0:t0 + Q]), 16)
                        inc(sW, s.dma_start(szb2[:, db * Q:(db + 1) * Q],
                                            sz_s.ap()[db * 128:(db + 1) * 128, t0:t0 + Q]), 16)
                W_CBIG = sW.v
                X_WT = sX.v

                @block.sync
                def _(s, k=k, t0=t0):
                    for db in range(4, DB):
                        inc(sW, s.dma_start(delta_sb[:, db * Q:(db + 1) * Q],
                                            delta_s.ap()[db * 128:(db + 1) * 128, t0:t0 + Q]), 16)
                        inc(sW, s.dma_start(szb2[:, db * Q:(db + 1) * Q],
                                            sz_s.ap()[db * 128:(db + 1) * 128, t0:t0 + Q]), 16)
                W_CB2 = sW.v

                @block.gpsimd
                def _(g, k=k):
                    for s_ in range(NS):
                        inc(sG, g.dma_start(
                            cb_sb[:, s_ * Q:(s_ + 1) * Q],
                            bass.AP(xp_s, (17 + s_) * T + k * Q, [[0, 128], [1, Q]])), 16)
                G_CIM = sG.v

                y_user = 0
                xi_free = [0, 0]
                pending_red = []
                for db in range(DB):
                    xslot = db % 2

                    @block.sync
                    def _(s, db=db, xslot=xslot, t0=t0, xf=xi_free[xslot]):
                        if xf:
                            s.wait_ge(sR.h, xf)
                        inc(sW, s.dma_start(xib2[:, xslot * Q:(xslot + 1) * Q],
                                            xi_s.ap()[db * 128:(db + 1) * 128, t0:t0 + Q]), 16)
                    W_XIDB = sW.v

                    for g_ in range(NG):
                        pg = (db * NG + g_) % 2
                        HOF = pg * NG * Q
                        AOF = pg * NG * Q
                        @block.scalar
                        def _(sc, db=db, g_=g_,
                              WLD=(W_CBIG if db < 4 else W_CB2),
                              af=a_free_par[pg], AOF=AOF):
                            sc.wait_ge(sW.h, WLD)
                            if af:
                                sc.wait_ge(sV.h, af)
                            for s4 in range(NG):
                                s_ = g_ * NG + s4
                                act = sc.activation(a_sb[:, AOF + s4 * Q:AOF + (s4 + 1) * Q],
                                                    delta_sb[:, db * Q:(db + 1) * Q],
                                                    AF.Exp,
                                                    scale=a_rep[:, s_:s_ + 1])
                            inc(sS, act)
                        S_A = sS.v

                        hg0 = (db * NG + g_) * 2
                        set0, set1 = hg0 % 3, (hg0 + 1) % 3

                        @block.tensor
                        def _(t, db=db, g_=g_, W_CBIG=W_CBIG, X_WT=X_WT,
                              uf0=u_free_set[set0], uf1=u_free_set[set1],
                              set0=set0, set1=set1):
                            t.wait_ge(sW.h, W_CBIG)
                            t.wait_ge(sX.h, X_WT)
                            for hh, st, uf in ((0, set0, uf0), (1, set1, uf1)):
                                if uf:
                                    t.wait_ge(sV.h, uf)
                                for ts_ in range(NTS):
                                    sl = ((ts_ * NG + g_) * 2 + hh) * 256
                                    mm = t.matmul(
                                        pU[:, st * 1024 + ts_ * 256:st * 1024 + (ts_ + 1) * 256],
                                        wt_sb[:, ts_ * DI + db * 128:ts_ * DI + (db + 1) * 128],
                                        bd_sb[:, sl:sl + 256],
                                        start=True, stop=True)
                            inc(sT, mm)
                        T_U = sT.v

                        @block.vector
                        def _(v, db=db, g_=g_, S_A=S_A, T_U=T_U,
                              hf=h_free_par[pg], HOF=HOF, AOF=AOF,
                              set0=set0, set1=set1):
                            v.wait_ge(sS.h, S_A)
                            v.wait_ge(sT.h, T_U)
                            if hf:
                                v.wait_ge(sP.h, hf)
                            for s4 in range(NG):
                                s_ = g_ * NG + s4
                                st = (set0, set1)[s4 // 2]
                                tts = v.tensor_tensor_scan(
                                    h_sb[:, HOF + s4 * Q:HOF + (s4 + 1) * Q],
                                    a_sb[:, AOF + s4 * Q:AOF + (s4 + 1) * Q],
                                    pU[:, st * 1024 + (s4 % 2):st * 1024 + 1024:2],
                                    state_sb[:, db * NS + s_:db * NS + s_ + 1],
                                    ALU.mult, ALU.add)
                                if s4 == 1:
                                    inc(sV, tts)
                            sv = v.tensor_copy(
                                state_sb[:, db * NS + g_ * NG:db * NS + g_ * NG + NG],
                                h_sb[:, HOF + Q - 1:HOF + NG * Q:Q])
                            inc(sV, sv)
                        a_free_par[pg] = sV.v
                        u_free_set[set0] = sV.v - 1
                        u_free_set[set1] = sV.v
                        V_SC = sV.v

                        @block.gpsimd
                        def _(g, db=db, g_=g_, V_SC=V_SC, yrp=yred_par[pg],
                              G_CIM=G_CIM, HOF=HOF):
                            g.wait_ge(sV.h, V_SC)
                            g.wait_ge(sG.h, G_CIM)
                            if yrp:
                                g.wait_ge(sR.h, yrp)
                            for s4 in range(NG):
                                s_ = g_ * NG + s4
                                tt = g.tensor_tensor(
                                    p_sb[:, HOF + s4 * Q:HOF + (s4 + 1) * Q],
                                    h_sb[:, HOF + s4 * Q:HOF + (s4 + 1) * Q],
                                    cb_sb[:, s_ * Q:(s_ + 1) * Q],
                                    ALU.mult)
                            inc(sP, tt)
                        h_free_par[pg] = sP.v
                        V_P = sP.v

                        def mk_reduce(db=db, g_=g_, V_P=V_P, yu=y_user,
                                      HOF=HOF, xslot=xslot, W_XIDB=W_XIDB):
                            def red(t):
                                t.wait_ge(sP.h, V_P)
                                if g_ == NG - 1:
                                    t.wait_ge(sW.h, W_XIDB)
                                if g_ == 0 and yu:
                                    t.wait_ge(sY.h, yu)
                                mms = []
                                for s4 in range(NG):
                                    mms.append(t.matmul(
                                        pY[:, :], idb[:],
                                        p_sb[:, HOF + s4 * Q:HOF + (s4 + 1) * Q],
                                        start=(g_ == 0 and s4 == 0), stop=False))
                                if g_ == NG - 1:
                                    mms.append(t.matmul(
                                        pY[:, :], dd_sb[:, db * 128:(db + 1) * 128],
                                        xib2[:, xslot * Q:(xslot + 1) * Q],
                                        start=False, stop=True))
                                instr = mms[-1]
                                instr.then_inc(sR.h, 1)
                            return red
                        if _COUNTING[0]:
                            sR.v += 1
                        T_Y = sR.v
                        yred_par[pg] = T_Y
                        if g_ == NG - 1:
                            xi_free[xslot] = T_Y
                        red_fn = mk_reduce()
                        if len(pending_red) >= 2:
                            pr = pending_red.pop(0)
                            block.tensor(pr[0])
                            if pr[1] is not None:
                                block.vector(pr[1])
                        ya_fn = None
                        if g_ == NG - 1:
                            def mk_ya(db=db, T_Y=T_Y, YOF=(k % 2) * DB * Q):
                                def ya(v):
                                    v.wait_ge(sR.h, T_Y)
                                    v.tensor_tensor(
                                        ya_sb[:, YOF + db * Q:YOF + (db + 1) * Q],
                                        pY[:, :],
                                        szb2[:, db * Q:(db + 1) * Q],
                                        ALU.mult).then_inc(sY.h, 1)
                                return ya
                            if _COUNTING[0]:
                                sY.v += 1
                            y_user = sY.v
                            ya_fn = mk_ya()
                        pending_red.append((red_fn, ya_fn))
                    if pending_wout[0] is not None and db < MB:
                        pt0, pya, pyof = pending_wout[0]
                        emit_wout_stage(db, pt0, pya, pyof)
                        if db == MB - 1:
                            pending_wout[0] = None

                tail_prev = (sS.v, sT.v, sY.v)
                V_YA = y_user
                for pr in pending_red:
                    block.tensor(pr[0])
                    if pr[1] is not None:
                        block.vector(pr[1])
                pending_red = []

                pending_wout[0] = (t0, V_YA, (k % 2) * DB * Q)
                if k == NCH - 1:
                    for mo in range(MB):
                        emit_wout_stage(mo, t0, V_YA, (k % 2) * DB * Q)

            @block.sync
            def _(s, wv=sW.v):
                s.wait_ge(sW.h, wv)

            @block.gpsimd
            def _(g, gv=sG.v):
                g.wait_ge(sG.h, gv)

            block.flush()

    return nc


_NC_CACHE = {}


def _get_nc():
    if "nc" not in _NC_CACHE:
        _NC_CACHE["nc"] = build_core_program()
    return _NC_CACHE["nc"]


def kernel(L0_attn_x, L0_attn_W_in, L0_attn_conv_w, L0_attn_conv_b, L0_attn_W_x,
           L0_attn_W_dt, L0_attn_b_dt, L0_attn_A_log, L0_attn_D, L0_attn_W_out,
           L0_ffn_W_r, L0_ffn_W_k, L0_ffn_W_v, L0_ffn_mu_r, L0_ffn_mu_k):
    x = np.asarray(L0_attn_x, dtype=np.float32)
    Bsz = x.shape[0]
    f32 = np.float32

    def cols(vec, nb):
        return np.ascontiguousarray(
            np.asarray(vec, f32).reshape(nb, 128).T)

    shared = dict(
        w_in=np.ascontiguousarray(np.asarray(L0_attn_W_in, f32)),
        w_rT=np.ascontiguousarray(np.asarray(L0_ffn_W_r, f32).T),
        w_kT=np.ascontiguousarray(np.asarray(L0_ffn_W_k, f32).T),
        w_vT=np.ascontiguousarray(np.asarray(L0_ffn_W_v, f32).T),
        w_out=np.ascontiguousarray(np.asarray(L0_attn_W_out, f32)),
        w_x=np.ascontiguousarray(np.asarray(L0_attn_W_x, f32)),
        wdt_row=np.ascontiguousarray(np.asarray(L0_attn_W_dt, f32).reshape(1, DI)),
        bdt_cols=cols(L0_attn_b_dt, DB),
        convb_cols=cols(L0_attn_conv_b, DB),
        d_cols=cols(L0_attn_D, DB),
        convw_cols=np.ascontiguousarray(np.concatenate(
            [cols(np.asarray(L0_attn_conv_w, f32)[:, 0, kk], DB) for kk in range(4)],
            axis=1)),
        mur_cols=cols(L0_ffn_mu_r, MB),
        muk_cols=cols(L0_ffn_mu_k, MB),
        alog_row=np.ascontiguousarray(np.asarray(L0_attn_A_log, f32).reshape(1, NS)),
    )
    in_maps = []
    for b in range(Bsz):
        m = dict(shared)
        m["xT"] = np.ascontiguousarray(x[b].T)
        in_maps.append(m)

    nc = _get_nc()
    res = run_bass_kernel_spmd(nc, in_maps, core_ids=list(range(Bsz)))
    out = np.stack([np.asarray(r["outT"]).T for r in res.results], axis=0)
    return np.ascontiguousarray(out.astype(np.float32))


# version 18 (final)
# speedup vs baseline: 1.1588x; 1.1588x over previous
import numpy as np
from contextlib import ExitStack

import concourse.bass as bass
import concourse.mybir as mybir
from concourse.bass_utils import run_bass_kernel_spmd

F32 = mybir.dt.float32
BF16 = mybir.dt.bfloat16
F32R = mybir.dt.float32r
AF = mybir.ActivationFunctionType
ALU = mybir.AluOpType

T = 8192
Q = 512
NCH = T // Q
D = 1024
DI = 2048
NS = 16
MB = D // 128
DB = DI // 128
NG = 4
TSUB = 128
NTS = Q // TSUB

DEBUG_SCRATCH = False


class Ctr:

    def __init__(self, stack, nc, name):
        cm = nc.semaphore(name)
        self.h = cm.__enter__()
        stack.callback(cm.__exit__, None, None, None)
        self.v = 0

    def tick(self, n=1):
        self.v += n
        return self.v


_COUNTING = [True]


def inc(ctr, instr, n=1):
    instr.then_inc(ctr.h, n)
    if _COUNTING[0]:
        ctr.v += n
    return ctr.v


class _MockInstr:
    def then_inc(self, *a, **k):
        return self


class _MockEng:

    def __getattr__(self, name):
        def m(*a, **k):
            return _MockInstr()
        return m


_MOCK = _MockEng()

_ENGS = ("sync", "gpsimd", "vector", "scalar", "tensor")


class _Recorder:

    def __init__(self, real_block):
        self._rb = real_block
        self.q = {e: [] for e in _ENGS}

    def _mk(self, eng):
        def dec(fn):
            self.q[eng].append(fn)
            fn(_MOCK)
        return dec

    @property
    def sync(self):
        return self._mk("sync")

    @property
    def gpsimd(self):
        return self._mk("gpsimd")

    @property
    def vector(self):
        return self._mk("vector")

    @property
    def scalar(self):
        return self._mk("scalar")

    @property
    def tensor(self):
        return self._mk("tensor")

    def flush(self):
        _COUNTING[0] = False
        try:
            for name in _ENGS:
                fns = self.q[name]
                if not fns:
                    continue
                dec = getattr(self._rb, name)

                def _section(eng, fns=tuple(fns)):
                    for f in fns:
                        f(eng)
                dec(_section)
                self.q[name] = []
        finally:
            _COUNTING[0] = True


def build_core_program():
    nc = bass.Bass("TRN2", target_bir_lowering=False,
                   detect_race_conditions=False)

    xT = nc.dram_tensor("xT", [D, T], F32, kind="ExternalInput")
    w_in = nc.dram_tensor("w_in", [D, 2 * DI], F32, kind="ExternalInput")
    w_rT = nc.dram_tensor("w_rT", [D, D], F32, kind="ExternalInput")
    w_kT = nc.dram_tensor("w_kT", [D, D], F32, kind="ExternalInput")
    w_vT = nc.dram_tensor("w_vT", [D, D], F32, kind="ExternalInput")
    w_out = nc.dram_tensor("w_out", [DI, D], F32, kind="ExternalInput")
    w_x = nc.dram_tensor("w_x", [DI, 33], F32, kind="ExternalInput")
    wdt_row = nc.dram_tensor("wdt_row", [1, DI], F32, kind="ExternalInput")
    bdt_cols = nc.dram_tensor("bdt_cols", [128, DB], F32, kind="ExternalInput")
    convb_cols = nc.dram_tensor("convb_cols", [128, DB], F32, kind="ExternalInput")
    d_cols = nc.dram_tensor("d_cols", [128, DB], F32, kind="ExternalInput")
    convw_cols = nc.dram_tensor("convw_cols", [128, 4 * DB], F32, kind="ExternalInput")
    mur_cols = nc.dram_tensor("mur_cols", [128, MB], F32, kind="ExternalInput")
    muk_cols = nc.dram_tensor("muk_cols", [128, MB], F32, kind="ExternalInput")
    alog_row = nc.dram_tensor("alog_row", [1, NS], F32, kind="ExternalInput")

    outT = nc.dram_tensor("outT", [D, T], F32, kind="ExternalOutput")

    sk = "ExternalOutput" if DEBUG_SCRATCH else "Internal"
    xn_s = nc.dram_tensor("xn_s", [D, T], BF16, kind=sk)
    outp_s = nc.dram_tensor("outp_s", [D, T], F32, kind=sk)
    xi_s = nc.dram_tensor("xi_s", [DI, T], BF16, kind=sk)
    sz_s = nc.dram_tensor("sz_s", [DI, T], BF16, kind=sk)
    delta_s = nc.dram_tensor("delta_s", [DI, T], BF16, kind=sk)
    w_s = nc.dram_tensor("w_s", [DI, T], BF16, kind=sk)
    xp_s = nc.dram_tensor("xp_s", [33, T], F32, kind=sk)
    m_row_s = nc.dram_tensor("m_row_s", [1, T], F32, kind=sk)
    r_row_s = nc.dram_tensor("r_row_s", [1, T], F32, kind=sk)
    NIMG = NCH * NTS * NG * 2
    bd_s = nc.dram_tensor("bd_s", [NIMG, 128 * 256], BF16, kind=sk)
    id_s = nc.dram_tensor("id_s", [128, 128], F32, kind=sk)
    dd_s = nc.dram_tensor("dd_s", [DB, 128 * 128], BF16, kind=sk)
    convd_s = nc.dram_tensor("convd_s", [4 * DB, 128 * 128], BF16, kind=sk)

    with ExitStack() as S:
        E = S.enter_context
        block = E(nc.Block())

        sW = Ctr(S, nc, "sW")
        sG = Ctr(S, nc, "sG")
        sV = Ctr(S, nc, "sV")
        sS = Ctr(S, nc, "sS")
        sT = Ctr(S, nc, "sT")
        sP = Ctr(S, nc, "sP")
        sY = Ctr(S, nc, "sY")
        sR = Ctr(S, nc, "sR")
        sX = Ctr(S, nc, "sX")
        bar = Ctr(S, nc, "bar")

        onesinv_col = E(nc.sbuf_tensor("onesinv_col", [128, 1], BF16))
        a_rep = E(nc.sbuf_tensor("a_rep", [128, NS], F32))
        idb = E(nc.sbuf_tensor("idb", [128, 128], BF16))
        colbuf = E(nc.sbuf_tensor("colbuf", [128, 6 * DB], F32))
        OF_BDT, OF_CVB, OF_D = 0, DB, 2 * DB
        OF_MUR, OF_MUK = 3 * DB, 3 * DB + MB
        OF_OMR, OF_OMK = 3 * DB + 2 * MB, 3 * DB + 3 * MB
        wdtr_sb = E(nc.sbuf_tensor("wdtr_sb", [1, DI], F32R))
        nopt = E(nc.sbuf_tensor("nopt", [1, 2], F32))

        with ExitStack() as SP:
            EP = SP.enter_context
            zero_big = EP(nc.sbuf_tensor("zero_big", [128, 2048], F32))
            colbf = EP(nc.sbuf_tensor("colbf", [128, 5 * DB], BF16))
            idcol = EP(nc.sbuf_tensor("idcol", [128, 1], F32))
            idn_f = EP(nc.sbuf_tensor("idn_f", [128, 128], F32))
            wdt_f = EP(nc.sbuf_tensor("wdt_f", [1, DI], F32))

            @block.vector
            def _(v):
                v.memset(onesinv_col[:], 1.0 / 1024.0)
                v.memset(idcol[:], 1.0)
                v.memset(nopt[:], 0.0)
                inc(sV, v.memset(zero_big[:], 0.0))
            V_ZERO = sV.v

            @block.sync
            def _(s):
                inc(sW, s.dma_start(colbuf[:, OF_BDT:OF_BDT + DB], bdt_cols.ap()), 16)
                inc(sW, s.dma_start(colbuf[:, OF_CVB:OF_CVB + DB], convb_cols.ap()), 16)
                inc(sW, s.dma_start(colbuf[:, OF_D:OF_D + DB], d_cols.ap()), 16)
                inc(sW, s.dma_start(colbuf[:, OF_MUR:OF_MUR + MB], mur_cols.ap()), 16)
                inc(sW, s.dma_start(colbuf[:, OF_MUK:OF_MUK + MB], muk_cols.ap()), 16)
                inc(sW, s.dma_start(wdt_f[:], wdt_row.ap()), 16)
                s.wait_ge(sV.h, V_ZERO)
                inc(sW, s.dma_start(id_s.ap(), zero_big[:, 0:128]), 16)
                zb = zero_big[:, :].bitcast(BF16)
                for img in range(0, NIMG, 16):
                    inc(sW, s.dma_start(
                        bass.AP(bd_s, img * 32768, [[4096, 128], [1, 4096]]), zb), 16)
                for j in range(2):
                    inc(sW, s.dma_start(
                        bass.AP(convd_s, j * 524288, [[4096, 128], [1, 4096]]), zb), 16)
                inc(sW, s.dma_start(
                    bass.AP(dd_s, 0, [[4096, 64], [1, 4096]]),
                    zero_big[0:64, :].bitcast(BF16)), 16)
            W_PRE = sW.v

            @block.gpsimd
            def _(g):
                inc(sG, g.dma_start(
                    a_rep[:], bass.AP(alog_row, 0, [[0, 128], [1, NS]])), 16)
                g.wait_ge(sW.h, W_PRE)
                inc(sG, g.dma_start(colbf[:, 0:DB], d_cols.ap()), 16)
                inc(sG, g.dma_start(colbf[:, DB:5 * DB], convw_cols.ap()), 16)
                inc(sG, g.dma_start(bass.AP(id_s, 0, [[129, 128], [1, 1]]),
                                    idcol[:, :], allow_slow_non_contiguous=True), 16)
                for db in range(DB):
                    inc(sG, g.dma_start(
                        bass.AP(dd_s, db * 16384, [[129, 128], [1, 1]]),
                        colbf[:, db:db + 1], allow_slow_non_contiguous=True), 16)
                for j in range(4 * DB):
                    inc(sG, g.dma_start(
                        bass.AP(convd_s, j * 16384, [[129, 128], [1, 1]]),
                        colbf[:, DB + j:DB + j + 1],
                        allow_slow_non_contiguous=True), 16)
            G_PRE = sG.v

            @block.sync
            def _(s):
                s.wait_ge(sG.h, G_PRE)
                inc(sW, s.dma_start(idn_f[:], id_s.ap()), 16)
            W_IDN = sW.v

            @block.scalar
            def _(sc):
                sc.wait_ge(sG.h, G_PRE)
                inc(sS, sc.activation(a_rep[:], a_rep[:], AF.Exp))
            S_AREP = sS.v

            @block.vector
            def _(v):
                v.wait_ge(sS.h, S_AREP)
                v.tensor_scalar_mul(a_rep[:], a_rep[:], -1.0)
                v.wait_ge(sW.h, W_IDN)
                v.tensor_copy(idb[:], idn_f[:])
                v.tensor_copy(wdtr_sb[:], wdt_f[:])
                v.tensor_scalar(colbuf[:, OF_OMR:OF_OMR + MB],
                                colbuf[:, OF_MUR:OF_MUR + MB],
                                -1.0, 1.0, ALU.mult, ALU.add)
                inc(sV, v.tensor_scalar(colbuf[:, OF_OMK:OF_OMK + MB],
                                        colbuf[:, OF_MUK:OF_MUK + MB],
                                        -1.0, 1.0, ALU.mult, ALU.add))

        def emit_barrier():
            tgt = bar.v + 5

            @block.sync
            def _(s, wv=sW.v, tgt=tgt):
                s.wait_ge(sW.h, wv)
                s.sem_inc(bar.h, 1)
                s.wait_ge(bar.h, tgt)

            @block.gpsimd
            def _(g, gv=sG.v, tgt=tgt):
                g.wait_ge(sG.h, gv)
                g.sem_inc(bar.h, 1)
                g.wait_ge(bar.h, tgt)

            @block.vector
            def _(v):
                v.sem_inc(bar.h, 1)
                v.wait_ge(bar.h, tgt)

            @block.scalar
            def _(sc):
                sc.sem_inc(bar.h, 1)
                sc.wait_ge(bar.h, tgt)

            @block.tensor
            def _(t):
                t.sem_inc(bar.h, 1)
                t.wait_ge(bar.h, tgt)
            bar.v = tgt

        emit_barrier()

        real_block = block
        block = _Recorder(real_block)

        with ExitStack() as SA:
            EA = SA.enter_context
            wr_sb = EA(nc.sbuf_tensor("wr_sb", [128, MB * D], BF16))
            wk_sb = EA(nc.sbuf_tensor("wk_sb", [128, MB * D], BF16))
            wv_sb = EA(nc.sbuf_tensor("wv_sb", [128, MB * D], BF16))
            x_sb = EA(nc.sbuf_tensor("x_sb", [128, MB * Q], F32))
            xbf = EA(nc.sbuf_tensor("xbf", [128, MB * Q], BF16))
            x2bf = EA(nc.sbuf_tensor("x2bf", [128, MB * Q], BF16))
            xn_sb = EA(nc.sbuf_tensor("xn_sb", [128, MB * (Q + 1)], BF16))
            xr_sb = EA(nc.sbuf_tensor("xr_sb", [128, MB * Q], BF16))
            xk_sb = EA(nc.sbuf_tensor("xk_sb", [128, MB * Q], BF16))
            r_sb = EA(nc.sbuf_tensor("r_sb", [128, MB * Q], F32))
            kact_sb = EA(nc.sbuf_tensor("kact_sb", [128, MB * Q], BF16))
            krelu = EA(nc.sbuf_tensor("krelu", [128, Q], F32))
            mb_b = EA(nc.sbuf_tensor("mb_b", [128, Q], F32))
            rb_b = EA(nc.sbuf_tensor("rb_b", [128, Q], F32))
            strow = EA(nc.sbuf_tensor("strow", [1, 5 * Q], F32))
            out_sb = EA(nc.sbuf_tensor("out_sb", [128, MB * Q], F32))
            tmpA = EA(nc.sbuf_tensor("tmpA", [128, Q], F32))
            pA = EA(nc.psum_tensor("pA", [128, 2 * Q], F32))
            pM = EA(nc.psum_tensor("pM", [1, Q], F32))
            pM2 = EA(nc.psum_tensor("pM2", [1, Q], F32))

            @block.gpsimd
            def _(g):
                for kt in range(MB):
                    inc(sG, g.dma_start(wr_sb[:, kt * D:(kt + 1) * D],
                                        w_rT.ap()[kt * 128:(kt + 1) * 128, :]), 16)
                    inc(sG, g.dma_start(wk_sb[:, kt * D:(kt + 1) * D],
                                        w_kT.ap()[kt * 128:(kt + 1) * 128, :]), 16)
                    inc(sG, g.dma_start(wv_sb[:, kt * D:(kt + 1) * D],
                                        w_vT.ap()[kt * 128:(kt + 1) * 128, :]), 16)
            G_AW = sG.v

            @block.vector
            def _(v):
                for mb in range(MB):
                    v.memset(xn_sb[:, mb * (Q + 1):mb * (Q + 1) + 1], 0.0)
                inc(sV, v.memset(tmpA[:], 0.0))
            V_AZ = sV.v

            ffn_bank_free = [0, 0]
            xn_store_tgt = [0]
            out_store_tgt = [0]
            x_consumed = [0]

            for k in range(NCH):
                t0 = k * Q

                @block.sync
                def _(s, k=k, t0=t0, xc=x_consumed[0]):
                    if k == 0:
                        s.wait_ge(sG.h, G_AW)
                        s.wait_ge(sV.h, V_AZ)
                    else:
                        s.wait_ge(sV.h, xc)
                    for mb in range(MB):
                        inc(sW, s.dma_start(x_sb[:, mb * Q:(mb + 1) * Q],
                                            xT.ap()[mb * 128:(mb + 1) * 128, t0:t0 + Q]), 16)
                W_X = sW.v

                @block.vector
                def _(v, W_X=W_X):
                    v.wait_ge(sW.h, W_X)
                    for mb in range(MB):
                        v.tensor_copy(xbf[:, mb * Q:(mb + 1) * Q],
                                      x_sb[:, mb * Q:(mb + 1) * Q])
                    inc(sV, v.engine_nop())
                V_XBF = sV.v

                @block.scalar
                def _(sc, V_XBF=V_XBF):
                    sc.wait_ge(sV.h, V_XBF)
                    for mb in range(MB - 1):
                        sc.activation(x2bf[:, mb * Q:(mb + 1) * Q],
                                      xbf[:, mb * Q:(mb + 1) * Q], AF.Square)
                    inc(sS, sc.activation(x2bf[:, (MB - 1) * Q:MB * Q],
                                          xbf[:, (MB - 1) * Q:MB * Q], AF.Square))
                S_X2 = sS.v

                @block.tensor
                def _(t, S_X2=S_X2):
                    t.wait_ge(sS.h, S_X2)
                    for mb in range(MB):
                        t.matmul(pM[:, :], onesinv_col[:], xbf[:, mb * Q:(mb + 1) * Q],
                                 start=(mb == 0), stop=(mb == MB - 1))
                    for mb in range(MB):
                        mm = t.matmul(pM2[:, :], onesinv_col[:],
                                      x2bf[:, mb * Q:(mb + 1) * Q],
                                      start=(mb == 0), stop=(mb == MB - 1))
                    inc(sT, mm)
                T_STAT = sT.v

                @block.vector
                def _(v, T_STAT=T_STAT):
                    v.wait_ge(sT.h, T_STAT)
                    m = strow[0:1, 0:Q]
                    var = strow[0:1, 2 * Q:3 * Q]
                    t1 = strow[0:1, 3 * Q:4 * Q]
                    v.tensor_copy(m, pM[:, :])
                    v.tensor_tensor(t1, m, m, ALU.mult)
                    v.tensor_tensor(var, pM2[:, :], t1, ALU.subtract)
                    inc(sV, v.tensor_scalar_add(var, var, 1e-5))
                V_VAR = sV.v

                @block.scalar
                def _(sc, V_VAR=V_VAR):
                    sc.wait_ge(sV.h, V_VAR)
                    var = strow[0:1, 2 * Q:3 * Q]
                    t1 = strow[0:1, 3 * Q:4 * Q]
                    sc.activation(t1, var, AF.Ln)
                    inc(sS, sc.activation(t1, t1, AF.Exp, scale=-0.5))
                S_RS = sS.v

                @block.vector
                def _(v, S_RS=S_RS):
                    v.wait_ge(sS.h, S_RS)
                    rr = strow[0:1, Q:2 * Q]
                    var = strow[0:1, 2 * Q:3 * Q]
                    r0 = strow[0:1, 3 * Q:4 * Q]
                    t2 = strow[0:1, 4 * Q:5 * Q]
                    v.tensor_tensor(t2, r0, r0, ALU.mult)
                    v.tensor_tensor(t2, t2, var, ALU.mult)
                    v.tensor_scalar(t2, t2, -0.5, 1.5, ALU.mult, ALU.add)
                    inc(sV, v.tensor_tensor(rr, r0, t2, ALU.mult))
                V_R = sV.v

                @block.sync
                def _(s, t0=t0, V_R=V_R):
                    s.wait_ge(sV.h, V_R)
                    inc(sW, s.dma_start(m_row_s.ap()[0:1, t0:t0 + Q], strow[0:1, 0:Q]), 16)
                    inc(sW, s.dma_start(r_row_s.ap()[0:1, t0:t0 + Q], strow[0:1, Q:2 * Q]), 16)
                W_MR = sW.v

                @block.gpsimd
                def _(g, t0=t0, W_MR=W_MR):
                    g.wait_ge(sW.h, W_MR)
                    inc(sG, g.dma_start(mb_b[:], bass.AP(m_row_s, t0, [[0, 128], [1, Q]])), 16)
                    inc(sG, g.dma_start(rb_b[:], bass.AP(r_row_s, t0, [[0, 128], [1, Q]])), 16)
                G_MR = sG.v

                @block.vector
                def _(v, G_MR=G_MR, k=k, xst=xn_store_tgt[0]):
                    v.wait_ge(sG.h, G_MR)
                    if k > 0:
                        v.wait_ge(sW.h, xst)
                    for mb in range(MB):
                        sl = xn_sb[:, mb * (Q + 1):(mb + 1) * (Q + 1)]
                        if k > 0:
                            v.tensor_copy(sl[:, 0:1], sl[:, Q:Q + 1])
                        v.tensor_tensor(tmpA[:], x_sb[:, mb * Q:(mb + 1) * Q], mb_b[:],
                                        ALU.subtract)
                        v.tensor_tensor(sl[:, 1:Q + 1], tmpA[:], rb_b[:], ALU.mult)
                    for mb in range(MB):
                        sl = xn_sb[:, mb * (Q + 1):(mb + 1) * (Q + 1)]
                        v.tensor_scalar_mul(xr_sb[:, mb * Q:(mb + 1) * Q], sl[:, 1:Q + 1],
                                            colbuf[:, OF_MUR + mb:OF_MUR + mb + 1])
                        v.scalar_tensor_tensor(xr_sb[:, mb * Q:(mb + 1) * Q], sl[:, 0:Q],
                                               colbuf[:, OF_OMR + mb:OF_OMR + mb + 1],
                                               xr_sb[:, mb * Q:(mb + 1) * Q],
                                               ALU.mult, ALU.add)
                        v.tensor_scalar_mul(xk_sb[:, mb * Q:(mb + 1) * Q], sl[:, 1:Q + 1],
                                            colbuf[:, OF_MUK + mb:OF_MUK + mb + 1])
                        v.scalar_tensor_tensor(xk_sb[:, mb * Q:(mb + 1) * Q], sl[:, 0:Q],
                                               colbuf[:, OF_OMK + mb:OF_OMK + mb + 1],
                                               xk_sb[:, mb * Q:(mb + 1) * Q],
                                               ALU.mult, ALU.add)
                    inc(sV, v.engine_nop())
                V_XN = sV.v

                @block.sync
                def _(s, t0=t0, V_XN=V_XN):
                    s.wait_ge(sV.h, V_XN)
                    for mb in range(MB):
                        inc(sW, s.dma_start(xn_s.ap()[mb * 128:(mb + 1) * 128, t0:t0 + Q],
                                            xn_sb[:, mb * (Q + 1) + 1:(mb + 1) * (Q + 1)]), 16)
                xn_store_tgt[0] = sW.v

                sig_done = {}
                for mo in range(MB):
                    bank = mo % 2

                    @block.tensor
                    def _(t, mo=mo, bank=bank, V_XN=V_XN,
                          free=ffn_bank_free[bank]):
                        t.wait_ge(sV.h, V_XN)
                        if free:
                            t.wait_ge(sS.h, free)
                        for kt in range(MB):
                            mm = t.matmul(pA[:, bank * Q:(bank + 1) * Q],
                                          wr_sb[:, kt * D + mo * 128:kt * D + (mo + 1) * 128],
                                          xr_sb[:, kt * Q:(kt + 1) * Q],
                                          start=(kt == 0), stop=(kt == MB - 1))
                        inc(sT, mm)
                    T_R = sT.v

                    @block.scalar
                    def _(sc, mo=mo, bank=bank, T_R=T_R):
                        sc.wait_ge(sT.h, T_R)
                        inc(sS, sc.activation(r_sb[:, mo * Q:(mo + 1) * Q],
                                              pA[:, bank * Q:(bank + 1) * Q], AF.Sigmoid))
                    ffn_bank_free[bank] = sS.v

                for mo in range(MB):
                    bank = mo % 2

                    @block.tensor
                    def _(t, mo=mo, bank=bank, free=ffn_bank_free[bank]):
                        t.wait_ge(sS.h, free)
                        for kt in range(MB):
                            mm = t.matmul(pA[:, bank * Q:(bank + 1) * Q],
                                          wk_sb[:, kt * D + mo * 128:kt * D + (mo + 1) * 128],
                                          xk_sb[:, kt * Q:(kt + 1) * Q],
                                          start=(kt == 0), stop=(kt == MB - 1))
                        inc(sT, mm)
                    T_K = sT.v

                    @block.scalar
                    def _(sc, mo=mo, bank=bank, T_K=T_K):
                        sc.wait_ge(sT.h, T_K)
                        sc.activation(krelu[:], pA[:, bank * Q:(bank + 1) * Q], AF.Relu)
                        inc(sS, sc.activation(kact_sb[:, mo * Q:(mo + 1) * Q], krelu[:],
                                              AF.Square))
                    ffn_bank_free[bank] = sS.v

                kact_all = max(ffn_bank_free)
                v_bank_free = [kact_all, kact_all]
                v_bank_is_vec = [False, False]
                for mo in range(MB):
                    bank = mo % 2

                    @block.tensor
                    def _(t, mo=mo, bank=bank, free=v_bank_free[bank],
                          isv=v_bank_is_vec[bank], ka=kact_all):
                        t.wait_ge(sS.h, ka)
                        t.wait_ge(sV.h if isv else sS.h, free)
                        for kt in range(MB):
                            mm = t.matmul(pA[:, bank * Q:(bank + 1) * Q],
                                          wv_sb[:, kt * D + mo * 128:kt * D + (mo + 1) * 128],
                                          kact_sb[:, kt * Q:(kt + 1) * Q],
                                          start=(kt == 0), stop=(kt == MB - 1))
                        inc(sT, mm)
                    T_V = sT.v

                    @block.vector
                    def _(v, mo=mo, bank=bank, T_V=T_V, ost=out_store_tgt[0]):
                        v.wait_ge(sT.h, T_V)
                        if ost:
                            v.wait_ge(sW.h, ost)
                        v.tensor_tensor(tmpA[:], r_sb[:, mo * Q:(mo + 1) * Q],
                                        pA[:, bank * Q:(bank + 1) * Q], ALU.mult)
                        inc(sV, v.tensor_tensor(out_sb[:, mo * Q:(mo + 1) * Q], tmpA[:],
                                                x_sb[:, mo * Q:(mo + 1) * Q], ALU.add))
                    v_bank_free[bank] = sV.v
                    v_bank_is_vec[bank] = True
                V_OUT = sV.v
                x_consumed[0] = V_OUT

                @block.sync
                def _(s, t0=t0, V_OUT=V_OUT):
                    s.wait_ge(sV.h, V_OUT)
                    for mb in range(MB):
                        inc(sW, s.dma_start(outp_s.ap()[mb * 128:(mb + 1) * 128, t0:t0 + Q],
                                            out_sb[:, mb * Q:(mb + 1) * Q]), 16)
                out_store_tgt[0] = sW.v

            emit_barrier()
            block.flush()

        with ExitStack() as SB:
            EB = SB.enter_context
            win_sb = EB(nc.sbuf_tensor("win_sb", [128, MB * 2 * DI], BF16))
            wx_sb = EB(nc.sbuf_tensor("wx_sb", [128, DB * 33], BF16))
            convd_sb = EB(nc.sbuf_tensor("convd_sb", [128, 4 * DB * 128], BF16))
            xnb_sb = EB(nc.sbuf_tensor("xnb_sb", [128, MB * Q], BF16))
            xipre = EB(nc.sbuf_tensor("xipre", [128, DB * (Q + 3)], BF16))
            xib_sb = EB(nc.sbuf_tensor("xib_sb", [128, DB * Q], BF16))
            szb = EB(nc.sbuf_tensor("szb", [128, 2 * Q], BF16))
            xp_sb = EB(nc.sbuf_tensor("xp_sb", [33, Q], F32))
            xp0r = EB(nc.sbuf_tensor("xp0r", [1, Q], F32R))
            xpt_sb = EB(nc.sbuf_tensor("xpt_sb", [128, NTS * 33], BF16))
            dtmp = EB(nc.sbuf_tensor("dtmp", [128, 2 * Q], F32))
            wtmp = EB(nc.sbuf_tensor("wtmp", [128, 2 * Q], BF16))
            pB = EB(nc.psum_tensor("pB", [128, 2 * Q], F32))
            pBc = EB(nc.psum_tensor("pBc", [128, 2 * Q], F32))
            pXp = EB(nc.psum_tensor("pXp", [33, Q], F32))
            pXpt = EB(nc.psum_tensor("pXpt", [128, NTS * 33], F32))

            @block.gpsimd
            def _(g):
                for kt in range(MB):
                    inc(sG, g.dma_start(win_sb[:, kt * 2 * DI:(kt + 1) * 2 * DI],
                                        w_in.ap()[kt * 128:(kt + 1) * 128, :]), 16)
                for kt in range(DB):
                    inc(sG, g.dma_start(wx_sb[:, kt * 33:(kt + 1) * 33],
                                        w_x.ap()[kt * 128:(kt + 1) * 128, :]), 16)
            G_BW = sG.v

            @block.sync
            def _(s):
                for j in range(4 * DB):
                    inc(sW, s.dma_start(
                        convd_sb[:, j * 128:(j + 1) * 128],
                        bass.AP(convd_s, j * 16384, [[128, 128], [1, 128]])), 16)
            W_BCD = sW.v

            @block.vector
            def _(v):
                for db in range(DB):
                    v.memset(xipre[:, db * (Q + 3):db * (Q + 3) + 3], 0.0)
                inc(sV, v.engine_nop())
            V_BZ = sV.v

            evac_tgt = [0, 0]
            cv_tgt = [0, 0]
            sz_store = [0, 0]
            d_store = [None, None]
            w_store = [0, 0]
            w_mult = [0, 0]
            xn_consumed = [0]
            bdw_tgt = [0]
            xi_store_tgt = [0]
            w_mult_last = [0]

            for k in range(NCH):
                t0 = k * Q

                @block.sync
                def _(s, k=k, t0=t0, xc=xn_consumed[0]):
                    if k == 0:
                        s.wait_ge(sG.h, G_BW)
                        s.wait_ge(sV.h, V_BZ)
                        s.wait_ge(sW.h, W_BCD)
                    else:
                        s.wait_ge(sT.h, xc)
                    for mb in range(MB):
                        inc(sW, s.dma_start(xnb_sb[:, mb * Q:(mb + 1) * Q],
                                            xn_s.ap()[mb * 128:(mb + 1) * 128, t0:t0 + Q]), 16)
                W_XN = sW.v

                @block.scalar
                def _(sc, k=k):
                    if k > 0:
                        for db in range(DB):
                            sl = xipre[:, db * (Q + 3):(db + 1) * (Q + 3)]
                            sc.copy(sl[:, 0:3], sl[:, Q:Q + 3])
                    inc(sS, sc.copy(nopt[0:1, 0:1], nopt[0:1, 1:2]))
                S_HIST = sS.v

                for o in range(2 * DB):
                    bank = o % 2

                    @block.tensor
                    def _(t, o=o, bank=bank, W_XN=W_XN, free=evac_tgt[bank]):
                        t.wait_ge(sW.h, W_XN)
                        if free:
                            t.wait_ge(sS.h, free)
                        for kt in range(MB):
                            mm = t.matmul(pB[:, bank * Q:(bank + 1) * Q],
                                          win_sb[:, kt * 2 * DI + o * 128:kt * 2 * DI + (o + 1) * 128],
                                          xnb_sb[:, kt * Q:(kt + 1) * Q],
                                          start=(kt == 0), stop=(kt == MB - 1))
                        inc(sT, mm)
                    T_O = sT.v

                    @block.scalar
                    def _(sc, o=o, bank=bank, T_O=T_O, S_HIST=S_HIST,
                          szst=sz_store[o % 2]):
                        sc.wait_ge(sT.h, T_O)
                        if o < DB:
                            sc.wait_ge(sS.h, S_HIST)
                            sl = xipre[:, o * (Q + 3):(o + 1) * (Q + 3)]
                            inc(sS, sc.copy(sl[:, 3:Q + 3], pB[:, bank * Q:(bank + 1) * Q]))
                        else:
                            if szst:
                                sc.wait_ge(sW.h, szst)
                            inc(sS, sc.activation(szb[:, (o % 2) * Q:(o % 2 + 1) * Q],
                                                  pB[:, bank * Q:(bank + 1) * Q], AF.Silu))
                    evac_tgt[bank] = sS.v

                    if o >= DB:
                        S_SZ = sS.v

                        @block.sync
                        def _(s, o=o, t0=t0, S_SZ=S_SZ):
                            s.wait_ge(sS.h, S_SZ)
                            db = o - DB
                            inc(sW, s.dma_start(
                                sz_s.ap()[db * 128:(db + 1) * 128, t0:t0 + Q],
                                szb[:, (o % 2) * Q:(o % 2 + 1) * Q]), 16)
                        sz_store[o % 2] = sW.v

                for db in range(DB):
                    bank = db % 2

                    @block.tensor
                    def _(t, db=db, bank=bank, free=cv_tgt[bank], ev=max(evac_tgt)):
                        t.wait_ge(sS.h, ev)
                        if free:
                            t.wait_ge(sS.h, free)
                        sl = xipre[:, db * (Q + 3):(db + 1) * (Q + 3)]
                        for j in range(4):
                            mm = t.matmul(pBc[:, bank * Q:(bank + 1) * Q],
                                          convd_sb[:, (j * DB + db) * 128:(j * DB + db + 1) * 128],
                                          sl[:, j:j + Q],
                                          start=(j == 0), stop=(j == 3))
                        inc(sT, mm)
                    T_CV = sT.v

                    @block.scalar
                    def _(sc, db=db, bank=bank, T_CV=T_CV,
                          xst=xi_store_tgt[0], wml=w_mult_last[0]):
                        sc.wait_ge(sT.h, T_CV)
                        if xst:
                            sc.wait_ge(sW.h, xst)
                        if wml:
                            sc.wait_ge(sV.h, wml)
                        inc(sS, sc.activation(xib_sb[:, db * Q:(db + 1) * Q],
                                              pBc[:, bank * Q:(bank + 1) * Q], AF.Silu,
                                              bias=colbuf[:, OF_CVB + db:OF_CVB + db + 1]))
                    cv_tgt[bank] = sS.v
                S_XI = sS.v

                @block.sync
                def _(s, t0=t0, S_XI=S_XI):
                    s.wait_ge(sS.h, S_XI)
                    for db in range(DB):
                        inc(sW, s.dma_start(xi_s.ap()[db * 128:(db + 1) * 128, t0:t0 + Q],
                                            xib_sb[:, db * Q:(db + 1) * Q]), 16)
                xi_store_tgt[0] = sW.v

                @block.tensor
                def _(t, S_XI=S_XI):
                    t.wait_ge(sS.h, S_XI)
                    for kt in range(DB):
                        mm = t.matmul(pXp[:, :], wx_sb[:, kt * 33:(kt + 1) * 33],
                                      xib_sb[:, kt * Q:(kt + 1) * Q],
                                      start=(kt == 0), stop=(kt == DB - 1))
                    inc(sT, mm)
                    for ts_ in range(NTS):
                        for kt in range(DB):
                            mm = t.matmul(pXpt[:, ts_ * 33:(ts_ + 1) * 33],
                                          xib_sb[:, kt * Q + ts_ * TSUB:kt * Q + (ts_ + 1) * TSUB],
                                          wx_sb[:, kt * 33:(kt + 1) * 33],
                                          start=(kt == 0), stop=(kt == DB - 1))
                    inc(sT, mm)
                T_XP = sT.v

                @block.scalar
                def _(sc, T_XP=T_XP, bdw=bdw_tgt[0]):
                    sc.wait_ge(sT.h, T_XP)
                    if bdw:
                        sc.wait_ge(sW.h, bdw)
                    sc.copy(xp_sb[:, :], pXp[:, :])
                    for ts_ in range(NTS):
                        inc(sS, sc.copy(xpt_sb[:, ts_ * 33:(ts_ + 1) * 33],
                                        pXpt[:, ts_ * 33:(ts_ + 1) * 33]))
                S_XP = sS.v

                @block.vector
                def _(v, S_XP=S_XP):
                    v.wait_ge(sS.h, S_XP)
                    inc(sV, v.tensor_copy(xp0r[:], xp_sb[0:1, :]))
                V_XP0 = sV.v

                @block.sync
                def _(s, t0=t0, S_XP=S_XP):
                    s.wait_ge(sS.h, S_XP)
                    inc(sW, s.dma_start(xp_s.ap()[:, t0:t0 + Q], xp_sb[:, :]), 16)

                @block.sync
                def _(s, k=k, S_XP=S_XP):
                    s.wait_ge(sS.h, S_XP)
                    for ts_ in range(NTS):
                        for gg in range(NG):
                            for hh in range(2):
                                img = ((k * NTS + ts_) * NG + gg) * 2 + hh
                                c0 = ts_ * 33 + 1 + 4 * gg + 2 * hh
                                inc(sW, s.dma_start(
                                    bass.AP(bd_s, img * 32768, [[258, 128], [1, 2]]),
                                    xpt_sb[:, c0:c0 + 2]), 16)
                bdw_tgt[0] = sW.v

                for db in range(DB):
                    slot = db % 2

                    @block.tensor
                    def _(t, db=db, slot=slot, V_XP0=V_XP0,
                          dst=d_store[slot], wm=w_mult[slot], cvt=max(cv_tgt)):
                        t.wait_ge(sV.h, V_XP0)
                        t.wait_ge(sS.h, cvt)
                        if wm:
                            t.wait_ge(sV.h, wm)
                        inc(sT, t.matmul(pBc[:, slot * Q:(slot + 1) * Q],
                                         wdtr_sb[0:1, db * 128:(db + 1) * 128],
                                         xp0r[:], start=True, stop=True))
                    T_DP = sT.v

                    @block.scalar
                    def _(sc, db=db, slot=slot, T_DP=T_DP, dst=d_store[slot]):
                        sc.wait_ge(sT.h, T_DP)
                        if dst:
                            sc.wait_ge(sW.h, dst[0])
                            sc.wait_ge(sG.h, dst[1])
                        sc.activation(dtmp[:, slot * Q:(slot + 1) * Q],
                                      pBc[:, slot * Q:(slot + 1) * Q], AF.Exp,
                                      bias=colbuf[:, OF_BDT + db:OF_BDT + db + 1])
                        inc(sS, sc.activation(dtmp[:, slot * Q:(slot + 1) * Q],
                                              dtmp[:, slot * Q:(slot + 1) * Q], AF.Ln,
                                              bias=1.0))
                    S_DL = sS.v

                    @block.vector
                    def _(v, db=db, slot=slot, S_DL=S_DL, wst=w_store[slot]):
                        v.wait_ge(sS.h, S_DL)
                        if wst:
                            v.wait_ge(sW.h, wst)
                        inc(sV, v.tensor_tensor(wtmp[:, slot * Q:(slot + 1) * Q],
                                                dtmp[:, slot * Q:(slot + 1) * Q],
                                                xib_sb[:, db * Q:(db + 1) * Q], ALU.mult))
                    w_mult[slot] = sV.v
                    V_W = sV.v

                    @block.gpsimd
                    def _(g, db=db, slot=slot, t0=t0, S_DL=S_DL):
                        g.wait_ge(sS.h, S_DL)
                        inc(sG, g.dma_start(delta_s.ap()[db * 128:(db + 1) * 128, t0:t0 + Q],
                                            dtmp[:, slot * Q:(slot + 1) * Q]), 16)
                    G_DST = sG.v

                    @block.sync
                    def _(s, db=db, slot=slot, t0=t0, V_W=V_W):
                        s.wait_ge(sV.h, V_W)
                        inc(sW, s.dma_start(w_s.ap()[db * 128:(db + 1) * 128, t0:t0 + Q],
                                            wtmp[:, slot * Q:(slot + 1) * Q]), 16)
                    d_store[slot] = (sW.v, G_DST)
                    w_store[slot] = sW.v

                xn_consumed[0] = sT.v
                w_mult_last[0] = sV.v

            emit_barrier()
            block.flush()

        with ExitStack() as SC:
            EC = SC.enter_context
            wout_sb = EC(nc.sbuf_tensor("wout_sb", [128, DB * D], BF16))
            dd_sb = EC(nc.sbuf_tensor("dd_sb", [128, DB * 128], BF16))
            delta_sb = EC(nc.sbuf_tensor("delta_sb", [128, DB * Q], BF16))
            xib2 = EC(nc.sbuf_tensor("xib2", [128, 2 * Q], BF16))
            szb2 = EC(nc.sbuf_tensor("szb2", [128, DB * Q], BF16))
            wt_sb = EC(nc.sbuf_tensor("wt_sb", [128, NTS * DI], BF16))
            bd_sb = EC(nc.sbuf_tensor("bd_sb", [128, NTS * NG * Q], BF16))
            cb_sb = EC(nc.sbuf_tensor("cb_sb", [128, NS * Q], BF16))
            a_sb = EC(nc.sbuf_tensor("a_sb", [128, 2 * NG * Q], F32))
            h_sb = EC(nc.sbuf_tensor("h_sb", [128, 2 * NG * Q], F32))
            p_sb = EC(nc.sbuf_tensor("p_sb", [128, 2 * NG * Q], BF16))
            ya_sb = EC(nc.sbuf_tensor("ya_sb", [128, 2 * DB * Q], BF16))
            outp_sb = EC(nc.sbuf_tensor("outp_sb", [128, 2 * Q], F32))
            outc_sb = EC(nc.sbuf_tensor("outc_sb", [128, 2 * Q], F32))
            state_sb = EC(nc.sbuf_tensor("state_sb", [128, DB * NS], F32))
            pU = EC(nc.psum_tensor("pU", [128, 3 * 1024], F32))
            pY = EC(nc.psum_tensor("pY", [128, Q], F32))
            pAt = EC(nc.psum_tensor("pAt", [128, Q], F32))

            @block.gpsimd
            def _(g):
                for kt in range(DB):
                    inc(sG, g.dma_start(wout_sb[:, kt * D:(kt + 1) * D],
                                        w_out.ap()[kt * 128:(kt + 1) * 128, :]), 16)
            G_CW = sG.v

            @block.sync
            def _(s):
                for db in range(DB):
                    inc(sW, s.dma_start(
                        dd_sb[:, db * 128:(db + 1) * 128],
                        bass.AP(dd_s, db * 16384, [[128, 128], [1, 128]])), 16)
            W_CD = sW.v

            @block.vector
            def _(v):
                inc(sV, v.memset(state_sb[:], 0.0))
            V_ST0 = sV.v

            out_store = [0, 0]
            h_free_par = [0, 0]
            yred_par = [0, 0]
            a_free_par = [0, 0]
            u_free_set = [0, 0, 0]
            tail_prev = None
            at_free = [0]
            op_free = [0, 0]
            pending_wout = [None]

            def emit_wout_stage(mo, t0k, V_YAk, YOFk):
                slot = mo % 2

                @block.sync
                def _(s, mo=mo, slot=slot, t0k=t0k, opf=op_free[slot]):
                    if opf:
                        s.wait_ge(sV.h, opf)
                    inc(sW, s.dma_start(outp_sb[:, slot * Q:(slot + 1) * Q],
                                        outp_s.ap()[mo * 128:(mo + 1) * 128, t0k:t0k + Q]), 16)
                W_OP = sW.v

                @block.tensor
                def _(t, mo=mo, V_YAk=V_YAk, YOFk=YOFk, atf=at_free[0]):
                    t.wait_ge(sY.h, V_YAk)
                    if atf:
                        t.wait_ge(sV.h, atf)
                    for kt in range(DB):
                        mm = t.matmul(pAt[:, :],
                                      wout_sb[:, kt * D + mo * 128:kt * D + (mo + 1) * 128],
                                      ya_sb[:, YOFk + kt * Q:YOFk + (kt + 1) * Q],
                                      start=(kt == 0), stop=(kt == DB - 1))
                    inc(sT, mm)
                T_AT = sT.v

                @block.vector
                def _(v, slot=slot, T_AT=T_AT, osf=out_store[slot], W_OP=W_OP):
                    v.wait_ge(sT.h, T_AT)
                    v.wait_ge(sW.h, W_OP)
                    if osf:
                        v.wait_ge(sW.h, osf)
                    inc(sV, v.tensor_tensor(outc_sb[:, slot * Q:(slot + 1) * Q],
                                            pAt[:, :],
                                            outp_sb[:, slot * Q:(slot + 1) * Q], ALU.add))
                V_OC = sV.v
                at_free[0] = V_OC
                op_free[slot] = V_OC

                @block.sync
                def _(s, mo=mo, slot=slot, t0k=t0k, V_OC=V_OC):
                    s.wait_ge(sV.h, V_OC)
                    inc(sW, s.dma_start(
                        outT.ap()[mo * 128:(mo + 1) * 128, t0k:t0k + Q],
                        outc_sb[:, slot * Q:(slot + 1) * Q]), 16)
                out_store[slot] = sW.v

            for k in range(NCH):
                t0 = k * Q

                @block.sync
                def _(s, k=k, t0=t0, tp=tail_prev):
                    if k == 0:
                        s.wait_ge(sG.h, G_CW)
                        s.wait_ge(sW.h, W_CD)
                        s.wait_ge(sV.h, V_ST0)
                    else:
                        s.wait_ge(sS.h, tp[0])
                        s.wait_ge(sT.h, tp[1])
                        s.wait_ge(sY.h, tp[2])
                    for ts_ in range(NTS):
                        inc(sX, s.dma_start_transpose(
                            wt_sb[:, ts_ * DI:(ts_ + 1) * DI],
                            w_s.ap()[:, t0 + ts_ * TSUB:t0 + (ts_ + 1) * TSUB]), 16)
                    for ts_ in range(NTS):
                        for gg in range(NG):
                            for hh in range(2):
                                img = ((k * NTS + ts_) * NG + gg) * 2 + hh
                                sl = ((ts_ * NG + gg) * 2 + hh) * 256
                                inc(sW, s.dma_start(
                                    bd_sb[:, sl:sl + 256],
                                    bass.AP(bd_s, img * 32768, [[256, 128], [1, 256]])), 16)
                    for db in range(4):
                        inc(sW, s.dma_start(delta_sb[:, db * Q:(db + 1) * Q],
                                            delta_s.ap()[db * 128:(db + 1) * 128, t0:t0 + Q]), 16)
                        inc(sW, s.dma_start(szb2[:, db * Q:(db + 1) * Q],
                                            sz_s.ap()[db * 128:(db + 1) * 128, t0:t0 + Q]), 16)
                W_CBIG = sW.v
                X_WT = sX.v

                @block.sync
                def _(s, k=k, t0=t0):
                    for db in range(4, DB):
                        inc(sW, s.dma_start(delta_sb[:, db * Q:(db + 1) * Q],
                                            delta_s.ap()[db * 128:(db + 1) * 128, t0:t0 + Q]), 16)
                        inc(sW, s.dma_start(szb2[:, db * Q:(db + 1) * Q],
                                            sz_s.ap()[db * 128:(db + 1) * 128, t0:t0 + Q]), 16)
                W_CB2 = sW.v

                @block.gpsimd
                def _(g, k=k):
                    for s_ in range(NS):
                        inc(sG, g.dma_start(
                            cb_sb[:, s_ * Q:(s_ + 1) * Q],
                            bass.AP(xp_s, (17 + s_) * T + k * Q, [[0, 128], [1, Q]])), 16)
                G_CIM = sG.v

                y_user = 0
                xi_free = [0, 0]
                pending_red = []
                for db in range(DB):
                    xslot = db % 2

                    @block.sync
                    def _(s, db=db, xslot=xslot, t0=t0, xf=xi_free[xslot]):
                        if xf:
                            s.wait_ge(sR.h, xf)
                        inc(sW, s.dma_start(xib2[:, xslot * Q:(xslot + 1) * Q],
                                            xi_s.ap()[db * 128:(db + 1) * 128, t0:t0 + Q]), 16)
                    W_XIDB = sW.v

                    for g_ in range(NG):
                        pg = (db * NG + g_) % 2
                        HOF = pg * NG * Q
                        AOF = pg * NG * Q
                        @block.scalar
                        def _(sc, db=db, g_=g_,
                              WLD=(W_CBIG if db < 4 else W_CB2),
                              af=a_free_par[pg], AOF=AOF):
                            sc.wait_ge(sW.h, WLD)
                            if af:
                                sc.wait_ge(sV.h, af)
                            for s4 in range(NG):
                                s_ = g_ * NG + s4
                                act = sc.activation(a_sb[:, AOF + s4 * Q:AOF + (s4 + 1) * Q],
                                                    delta_sb[:, db * Q:(db + 1) * Q],
                                                    AF.Exp,
                                                    scale=a_rep[:, s_:s_ + 1])
                            inc(sS, act)
                        S_A = sS.v

                        hg0 = (db * NG + g_) * 2
                        set0, set1 = hg0 % 3, (hg0 + 1) % 3

                        @block.tensor
                        def _(t, db=db, g_=g_, W_CBIG=W_CBIG, X_WT=X_WT,
                              uf0=u_free_set[set0], uf1=u_free_set[set1],
                              set0=set0, set1=set1):
                            t.wait_ge(sW.h, W_CBIG)
                            t.wait_ge(sX.h, X_WT)
                            for hh, st, uf in ((0, set0, uf0), (1, set1, uf1)):
                                if uf:
                                    t.wait_ge(sV.h, uf)
                                for ts_ in range(NTS):
                                    sl = ((ts_ * NG + g_) * 2 + hh) * 256
                                    mm = t.matmul(
                                        pU[:, st * 1024 + ts_ * 256:st * 1024 + (ts_ + 1) * 256],
                                        wt_sb[:, ts_ * DI + db * 128:ts_ * DI + (db + 1) * 128],
                                        bd_sb[:, sl:sl + 256],
                                        start=True, stop=True)
                            inc(sT, mm)
                        T_U = sT.v

                        @block.vector
                        def _(v, db=db, g_=g_, S_A=S_A, T_U=T_U,
                              hf=h_free_par[pg], HOF=HOF, AOF=AOF,
                              set0=set0, set1=set1):
                            v.wait_ge(sS.h, S_A)
                            v.wait_ge(sT.h, T_U)
                            if hf:
                                v.wait_ge(sP.h, hf)
                            for s4 in range(NG):
                                s_ = g_ * NG + s4
                                st = (set0, set1)[s4 // 2]
                                tts = v.tensor_tensor_scan(
                                    h_sb[:, HOF + s4 * Q:HOF + (s4 + 1) * Q],
                                    a_sb[:, AOF + s4 * Q:AOF + (s4 + 1) * Q],
                                    pU[:, st * 1024 + (s4 % 2):st * 1024 + 1024:2],
                                    state_sb[:, db * NS + s_:db * NS + s_ + 1],
                                    ALU.mult, ALU.add)
                                if s4 == 1:
                                    inc(sV, tts)
                            sv = v.tensor_copy(
                                state_sb[:, db * NS + g_ * NG:db * NS + g_ * NG + NG],
                                h_sb[:, HOF + Q - 1:HOF + NG * Q:Q])
                            inc(sV, sv)
                        a_free_par[pg] = sV.v
                        u_free_set[set0] = sV.v - 1
                        u_free_set[set1] = sV.v
                        V_SC = sV.v

                        @block.gpsimd
                        def _(g, db=db, g_=g_, V_SC=V_SC, yrp=yred_par[pg],
                              G_CIM=G_CIM, HOF=HOF):
                            g.wait_ge(sV.h, V_SC)
                            g.wait_ge(sG.h, G_CIM)
                            if yrp:
                                g.wait_ge(sR.h, yrp)
                            for s4 in range(NG):
                                s_ = g_ * NG + s4
                                tt = g.tensor_tensor(
                                    p_sb[:, HOF + s4 * Q:HOF + (s4 + 1) * Q],
                                    h_sb[:, HOF + s4 * Q:HOF + (s4 + 1) * Q],
                                    cb_sb[:, s_ * Q:(s_ + 1) * Q],
                                    ALU.mult)
                            inc(sP, tt)
                        h_free_par[pg] = sP.v
                        V_P = sP.v

                        def mk_reduce(db=db, g_=g_, V_P=V_P, yu=y_user,
                                      HOF=HOF, xslot=xslot, W_XIDB=W_XIDB):
                            def red(t):
                                t.wait_ge(sP.h, V_P)
                                if g_ == NG - 1:
                                    t.wait_ge(sW.h, W_XIDB)
                                if g_ == 0 and yu:
                                    t.wait_ge(sY.h, yu)
                                mms = []
                                for s4 in range(NG):
                                    mms.append(t.matmul(
                                        pY[:, :], idb[:],
                                        p_sb[:, HOF + s4 * Q:HOF + (s4 + 1) * Q],
                                        start=(g_ == 0 and s4 == 0), stop=False))
                                if g_ == NG - 1:
                                    mms.append(t.matmul(
                                        pY[:, :], dd_sb[:, db * 128:(db + 1) * 128],
                                        xib2[:, xslot * Q:(xslot + 1) * Q],
                                        start=False, stop=True))
                                instr = mms[-1]
                                instr.then_inc(sR.h, 1)
                            return red
                        if _COUNTING[0]:
                            sR.v += 1
                        T_Y = sR.v
                        yred_par[pg] = T_Y
                        if g_ == NG - 1:
                            xi_free[xslot] = T_Y
                        red_fn = mk_reduce()
                        if len(pending_red) >= 2:
                            pr = pending_red.pop(0)
                            block.tensor(pr[0])
                            if pr[1] is not None:
                                block.vector(pr[1])
                        ya_fn = None
                        if g_ == NG - 1:
                            def mk_ya(db=db, T_Y=T_Y, YOF=(k % 2) * DB * Q):
                                def ya(v):
                                    v.wait_ge(sR.h, T_Y)
                                    v.tensor_tensor(
                                        ya_sb[:, YOF + db * Q:YOF + (db + 1) * Q],
                                        pY[:, :],
                                        szb2[:, db * Q:(db + 1) * Q],
                                        ALU.mult).then_inc(sY.h, 1)
                                return ya
                            if _COUNTING[0]:
                                sY.v += 1
                            y_user = sY.v
                            ya_fn = mk_ya()
                        pending_red.append((red_fn, ya_fn))
                    if pending_wout[0] is not None and db < MB:
                        pt0, pya, pyof = pending_wout[0]
                        emit_wout_stage(db, pt0, pya, pyof)
                        if db == MB - 1:
                            pending_wout[0] = None

                tail_prev = (sS.v, sT.v, sY.v)
                V_YA = y_user
                for pr in pending_red:
                    block.tensor(pr[0])
                    if pr[1] is not None:
                        block.vector(pr[1])
                pending_red = []

                pending_wout[0] = (t0, V_YA, (k % 2) * DB * Q)
                if k == NCH - 1:
                    for mo in range(MB):
                        emit_wout_stage(mo, t0, V_YA, (k % 2) * DB * Q)

            @block.sync
            def _(s, wv=sW.v):
                s.wait_ge(sW.h, wv)

            @block.gpsimd
            def _(g, gv=sG.v):
                g.wait_ge(sG.h, gv)

            block.flush()

    return nc


_NC_CACHE = {}


def _get_nc():
    if "nc" not in _NC_CACHE:
        _NC_CACHE["nc"] = build_core_program()
    return _NC_CACHE["nc"]


def kernel(L0_attn_x, L0_attn_W_in, L0_attn_conv_w, L0_attn_conv_b, L0_attn_W_x,
           L0_attn_W_dt, L0_attn_b_dt, L0_attn_A_log, L0_attn_D, L0_attn_W_out,
           L0_ffn_W_r, L0_ffn_W_k, L0_ffn_W_v, L0_ffn_mu_r, L0_ffn_mu_k):
    x = np.asarray(L0_attn_x, dtype=np.float32)
    Bsz = x.shape[0]
    f32 = np.float32

    def cols(vec, nb):
        return np.ascontiguousarray(
            np.asarray(vec, f32).reshape(nb, 128).T)

    shared = dict(
        w_in=np.ascontiguousarray(np.asarray(L0_attn_W_in, f32)),
        w_rT=np.ascontiguousarray(np.asarray(L0_ffn_W_r, f32).T),
        w_kT=np.ascontiguousarray(np.asarray(L0_ffn_W_k, f32).T),
        w_vT=np.ascontiguousarray(np.asarray(L0_ffn_W_v, f32).T),
        w_out=np.ascontiguousarray(np.asarray(L0_attn_W_out, f32)),
        w_x=np.ascontiguousarray(np.asarray(L0_attn_W_x, f32)),
        wdt_row=np.ascontiguousarray(np.asarray(L0_attn_W_dt, f32).reshape(1, DI)),
        bdt_cols=cols(L0_attn_b_dt, DB),
        convb_cols=cols(L0_attn_conv_b, DB),
        d_cols=cols(L0_attn_D, DB),
        convw_cols=np.ascontiguousarray(np.concatenate(
            [cols(np.asarray(L0_attn_conv_w, f32)[:, 0, kk], DB) for kk in range(4)],
            axis=1)),
        mur_cols=cols(L0_ffn_mu_r, MB),
        muk_cols=cols(L0_ffn_mu_k, MB),
        alog_row=np.ascontiguousarray(np.asarray(L0_attn_A_log, f32).reshape(1, NS)),
    )
    in_maps = []
    for b in range(Bsz):
        m = dict(shared)
        m["xT"] = np.ascontiguousarray(x[b].T)
        in_maps.append(m)

    nc = _get_nc()
    res = run_bass_kernel_spmd(nc, in_maps, core_ids=list(range(Bsz)))
    out = np.stack([np.asarray(r["outT"]).T for r in res.results], axis=0)
    return np.ascontiguousarray(out.astype(np.float32))
